# version 13 (final)
# baseline (speedup 1.0000x reference)
import contextlib
import math
import bisect
import numpy as np
import ml_dtypes
import concourse.bass as bass
import concourse.mybir as mybir
from concourse.bass_utils import run_bass_kernel_spmd

F32 = mybir.dt.float32
BF16 = mybir.dt.bfloat16
AX = mybir.AxisListType.X
AF = mybir.ActivationFunctionType
ALU = mybir.AluOpType
BFNP = ml_dtypes.bfloat16

NCORES = 8
N = 2048
D = 1024
H = 16
DH = 64
DHEAVY = 4096
DLIGHT = 512
NCH = 16
EPS = 1e-5
KEFF = 1024 * 9.0 / 8.0
LOGK = float(np.log(np.float32(KEFF)))
LN32 = float(np.log(np.float32(32.0)))
NEG = -1.0e30

ENGINES = ("tensor", "vector", "scalar", "sync", "gpsimd")
DBG = False
DBG_FFN = False


class _Res:
    __slots__ = ("writes", "reads")

    def __init__(self):
        self.writes = {}
        self.reads = {}


class _Op:
    __slots__ = ("emit", "waits", "tok", "seq")

    def __init__(self, emit, waits, tok, seq):
        self.emit = emit
        self.waits = waits
        self.tok = tok
        self.seq = seq


class Prog:

    def __init__(self):
        self.ops = {e: [] for e in ENGINES}
        self.seq = {e: 0 for e in ENGINES}
        self.res = {}
        self.know = {e: {} for e in ENGINES}
        self.snap = {}
        self.dma_count = {}
        self.arena = []

    def _dep_tokens(self, reads, writes):
        deps = {}
        for r in reads:
            st = self.res.get(r)
            if st:
                for k, v in st.writes.items():
                    if deps.get(k, 0) < v:
                        deps[k] = v
        for w in writes:
            st = self.res.get(w)
            if st:
                for k, v in st.writes.items():
                    if deps.get(k, 0) < v:
                        deps[k] = v
                for k, v in st.reads.items():
                    if deps.get(k, 0) < v:
                        deps[k] = v
        return deps

    def op(self, e, emit, reads=(), writes=(), dma=None):
        deps = self._dep_tokens(reads, writes)
        know = self.know[e]
        waits = []
        for k, v in deps.items():
            if k == ("E", e) and e == "tensor":
                continue
            if know.get(k, 0) >= v:
                continue
            waits.append((k, v))
            know[k] = v
            sn = self.snap.get((k, v))
            if sn:
                for k2, v2 in sn.items():
                    if know.get(k2, 0) < v2:
                        know[k2] = v2
        if dma is None:
            self.seq[e] += 1
            tok = (("E", e), self.seq[e])
            seq = self.seq[e]
        else:
            prev = self.dma_count.get(dma, 0)
            if prev and know.get(("D", dma), 0) < prev:
                waits.append((("D", dma), prev))
                know[("D", dma)] = prev
                sn = self.snap.get((("D", dma), prev))
                if sn:
                    for k2, v2 in sn.items():
                        if know.get(k2, 0) < v2:
                            know[k2] = v2
            c = prev + 16
            self.dma_count[dma] = c
            tok = (("D", dma), c)
            seq = None
        self.snap[tok] = dict(know)
        self.snap[tok][("E", e)] = self.seq[e]
        self.ops[e].append(_Op(emit, waits, tok, seq))
        for r in reads:
            st = self.res.setdefault(r, _Res())
            if st.reads.get(tok[0], 0) < tok[1]:
                st.reads[tok[0]] = tok[1]
        for w in writes:
            st = self.res.setdefault(w, _Res())
            st.writes = {tok[0]: tok[1]}
            st.reads = {}
        return tok

    def arena_view(self, name, off, size):
        st = self.res.setdefault(name, _Res())
        for (oname, ooff, osize) in self.arena:
            if oname == name:
                continue
            if off < ooff + osize and ooff < off + osize:
                ost = self.res.get(oname)
                if ost:
                    for k, v in list(ost.writes.items()) + list(ost.reads.items()):
                        if st.writes.get(k, 0) < v:
                            st.writes[k] = v
        self.arena = [a for a in self.arena if a[0] != name] + [(name, off, size)]

    def finalize(self, nc):
        eng_sem = {e: nc.alloc_semaphore(f"sem_{e}") for e in ENGINES}
        dma_sem = {s: nc.alloc_semaphore(f"dma_{s}") for s in self.dma_count}
        waited = {e: set() for e in ENGINES}
        for e in ENGINES:
            for o in self.ops[e]:
                for k, v in o.waits:
                    if k[0] == "E":
                        waited[k[1]].add(v)
        wlist = {e: sorted(waited[e]) for e in ENGINES}

        def emit_stream(eng, ename):
            for o in self.ops[ename]:
                for k, v in o.waits:
                    if k[0] == "D":
                        eng.wait_ge(dma_sem[k[1]], v)
                    else:
                        nval = bisect.bisect_right(wlist[k[1]], v)
                        eng.wait_ge(eng_sem[k[1]], nval)
                inst = o.emit(eng)
                if o.seq is not None and o.seq in waited[ename]:
                    inst.then_inc(eng_sem[ename], 1)
                elif o.seq is None:
                    inst.then_inc(dma_sem[o.tok[0][1]], 16)
            if ename == "sync":
                for s, c in self.dma_count.items():
                    eng.wait_ge(dma_sem[s], c)

        block = nc.Block()
        block.__enter__()
        try:
            for ename in ENGINES:
                getattr(block, ename)(
                    (lambda eng, _n=ename: emit_stream(eng, _n)))
        finally:
            block.__exit__(None, None, None)



def build_nc(phases=("ffn1", "attn", "ffn2"), dbg_ln_chunks=NCH, dbg_skip=()):
    nc = bass.Bass("TRN2", target_bir_lowering=False, debug=False,
                   detect_race_conditions=False)
    P = Prog()

    dr = {}
    def dt_in(name, shape, dtype=F32):
        dr[name] = nc.dram_tensor(name, shape, dtype, kind="ExternalInput").ap()
    dt_in("x0", [NCH, 128, D])
    dt_in("wkqv", [H, 8, 128, 2112])
    dt_in("wp", [8, 128, D])
    dt_in("bkq", [128, 256])
    dt_in("bv", [1, H * DH])
    dt_in("w1l", [4, 128, 8, 128])
    dt_in("w2l", [4, 128, D])
    dt_in("w1h", [32, 128, 8, 128])
    dt_in("w2h", [32, 128, D])
    dt_in("b1h", [128, 32])
    dt_in("b1l", [128, 4])
    dt_in("b2l", [1, D])
    dt_in("b2h", [1, D])
    dt_in("bp", [1, D])
    dt_in("gl", [128, 8])
    dt_in("gh", [128, 8])
    dt_in("rrep", [128, D])
    dt_in("cmask", [128, 896], BF16)
    dt_in("dneg", [128, 128], BF16)
    dt_in("i128bf", [128, 128], BF16)
    dt_in("i128f", [128, 128])
    dt_in("ones1bf", [1, 128], BF16)
    dt_in("ones1f", [1, 128])
    dt_in("ones128f", [128, 128])
    dt_in("l128f", [128, 128])
    dt_in("i16f", [16, 16])
    dt_in("l16f", [16, 16])
    out_dr = nc.dram_tensor("out", [NCH, 128, D], F32, kind="ExternalOutput").ap()
    if DBG:
        dbg_dr = nc.dram_tensor("dbg", [64, 128, 2048], F32, kind="ExternalOutput").ap()
    dbg_i = [0]
    def dump(buf_ap, res_list, cols):
        i = dbg_i[0]; dbg_i[0] += 1
        P.op("vector", lambda eng, buf_ap=buf_ap, cols=cols: eng.tensor_copy(
            T["dbgstage"][:, 0:cols], buf_ap), reads=list(res_list) + ["dbgstage"],
            writes=["dbgstage"])
        dma("sync", "dbgst", dbg_dr[i], T["dbgstage"][:],
            ["dbgstage"], [f"dbg{i}"])
    x1_dr = nc.dram_tensor("x1buf", [NCH, 128, D], F32, kind="Internal").ap()
    x2_dr = nc.dram_tensor("x2buf", [NCH, 128, D], F32, kind="Internal").ap()
    w1hbf_dr = nc.dram_tensor("w1hbf", [32, 128, D], BF16, kind="Internal").ap()
    w2hbf_dr = nc.dram_tensor("w2hbf", [32, 128, D], BF16, kind="Internal").ap()

    sb = {}
    def alloc(name, shape, dtype=F32):
        sb[name] = nc.alloc_sbuf_tensor(name, shape, dtype)
        return sb[name]

    alloc("act", [128, 8, N], BF16)
    alloc("xs0", [128, D]); alloc("xs1", [128, D])
    alloc("xr0", [128, D]); alloc("xr1", [128, D])
    alloc("ln0", [128, D])
    alloc("ou0", [128, D]); alloc("ou1", [128, D])
    for nm in ("sumx", "ssq", "mu", "var", "rstd", "tsc", "t16a", "t16b",
               "s16", "sb16", "u16", "z16", "u01", "cg16", "ci16", "mask16"):
        alloc(nm, [128, 16])
    for nm in ("junk1", "part1", "a1", "c1", "l1", "m0", "m1", "negm", "S1",
               "rS", "cp0", "cp1", "d0", "d1", "sp0", "sp1", "sp2", "sp3"):
        alloc(nm, [128, 1])
    alloc("db0", [128, 128], BF16); alloc("db1", [128, 128], BF16)
    alloc("junk16", [128, 16])
    alloc("t16c", [16, 1]); alloc("offs", [1, 16])
    alloc("c_i128bf", [128, 128], BF16)
    alloc("c_i128f", [128, 128])
    alloc("c_ones1bf", [1, 128], BF16)
    alloc("c_ones1f", [1, 128])
    alloc("c_ones128f", [128, 128])
    alloc("c_l128f", [128, 128])
    alloc("c_i16f", [16, 16])
    alloc("c_l16f", [16, 16])
    alloc("c_cm", [128, 896], BF16)
    alloc("c_dneg", [128, 128], BF16)
    alloc("c_bkq", [128, 256])
    alloc("c_b1h", [128, 32])
    alloc("c_b1l", [128, 4])
    alloc("c_gl", [128, 8])
    alloc("c_gh", [128, 8])
    alloc("c_b2l", [1, D], BF16)
    alloc("c_b2h", [1, D], BF16)
    alloc("c_bp", [1, D], BF16)
    alloc("c_bv", [1, H * DH], BF16)

    ARENA = 117760
    F32ARENA = 16896
    arena_t = nc.alloc_sbuf_tensor("arena", [128, ARENA // 2], BF16)
    arena_f = nc.alloc_sbuf_tensor("arenaf", [128, F32ARENA // 4], F32)
    def aview(name, off, shape, dtype):
        nel = int(np.prod(shape[1:]))
        isz = 2 if dtype == BF16 else 4
        nbytes = nel * isz
        if dtype == BF16:
            ap = arena_t.ap()[:, off // 2:(off + nbytes) // 2]
        else:
            ap = arena_f.ap()[:, off // 4:(off + nbytes) // 4]
        if len(shape) == 3:
            ap = ap.rearrange("p (a b) -> p a b", a=shape[1])
        elif len(shape) == 4:
            ap = ap.rearrange("p (a b c) -> p a b c", a=shape[1], b=shape[2])
        if shape[0] != 128:
            ap = ap[0:shape[0]]
        sb[name] = ap
        return off + nbytes
    aview("h1", 0, [128, 32, 512], BF16)
    aview("w1s", 32768, [128, 2, D], BF16)
    aview("w2s", 36864, [128, 2, D], BF16)
    aview("h1l", 40960, [128, 4, 512], BF16)
    aview("lw1", 45056, [128, 4, 8, 128], BF16)
    aview("lw2", 53248, [128, 4, D], BF16)
    aview("lout", 61440, [128, 4, D], BF16)
    aview("junk2k", 69632, [128, N], BF16)
    aview("c_gfhb", 73728, [128, D], BF16)
    aview("K", 0, [128, 8, N], BF16)
    aview("sa", 32768, [128, 8, N], BF16)
    aview("wkv", 65536, [128, 8, 2112], BF16)
    aview("wpb", 65536, [128, 8, D], BF16)
    aview("Q", 99328, [128, 8, 512], BF16)
    aview("V", 107520, [128, 16, DH], BF16)
    aview("Pb", 109568, [128, N], BF16)
    aview("PT", 113664, [128, N], BF16)
    aview("ys", 113664, [128, D], BF16)
    aview("zR", 0, [128, N], F32)
    aview("tmpc", 0, [128, D], F32)
    aview("c_rrep", 8192, [128, D], F32)
    aview("zflat", 8192, [1, N], F32)
    aview("zt", 16384, [16, 128], F32)
    if DBG:
        aview("dbgstage", 0, [128, 2048], F32)
    FFN_VIEWS = {"h1": (0, 32768), "w1s": (32768, 4096), "w2s": (36864, 4096),
                 "h1l": (40960, 4096), "lw1": (45056, 8192), "lw2": (53248, 8192),
                 "lout": (61440, 8192), "junk2k": (69632, 4096),
                 "ys": (113664, 2048)}
    F32_VIEWS = {"zR": (1 << 24, 8192), "tmpc": (1 << 24, 4096),
                 "c_rrep": ((1 << 24) + 8192, 4096), "zt": ((1 << 24) + 16384, 512)}
    ZFLAT_VIEW = ((1 << 24) + 8192, 8192)
    ATTN_VIEWS = {"K": (0, 32768), "sa": (32768, 32768), "wkv": (65536, 33792),
                  "Q": (99328, 8192), "V": (107520, 2048), "Pb": (109568, 4096),
                  "PT": (113664, 4096)}
    WP_VIEW = {"wpb": (65536, 16384)}

    pband = []
    for i in range(8):
        pband.append(nc.alloc_psum_tensor(f"ps{i}", [128, 512], F32))

    class Pool:
        def __init__(self, idxs):
            self.idxs = idxs
            self.i = 0
        def alloc(self):
            j = self.idxs[self.i % len(self.idxs)]
            self.i += 1
            return pband[j], f"ps{j}"
    pool = Pool([0, 1, 2, 3, 4])
    ptpool = Pool([5, 6])

    T = sb
    def dma(e, stream, dst, src, reads, writes):
        P.op(e, lambda eng, d=dst, s=src: eng.dma_start(out=d, in_=s),
             reads=reads, writes=writes, dma=stream)

    for nm, srcn in (("c_ones128f", "ones128f"), ("c_l128f", "l128f"),
                    ("c_i16f", "i16f"), ("c_l16f", "l16f"), ("c_bkq", "bkq"),
                    ("c_b1h", "b1h"), ("c_b1l", "b1l"), ("c_gl", "gl"),
                    ("c_i128f", "i128f"), ("c_ones1f", "ones1f")):
        dma("sync", "cst", T[nm][:], dr[srcn], [], [nm])
    dma("sync", "cst", T["c_cm"][:], dr["cmask"], [], ["c_cm"])
    dma("sync", "cst", T["c_i128bf"][:], dr["i128bf"], [], ["c_i128bf"])
    dma("sync", "cst", T["c_ones1bf"][:], dr["ones1bf"], [], ["c_ones1bf"])
    for nm, src in (("c_b2l", "b2l"), ("c_b2h", "b2h"), ("c_bp", "bp"),
                    ("c_bv", "bv")):
        dma("gpsimd", "lw", T[nm][:], dr[src], [], [nm])
    P.arena_view("c_gfhb", 73728, 2048)
    dma("sync", "cst", T["c_gh"][:], dr["gh"], [], ["c_gh"])
    P.op("vector", lambda eng: eng.memset(T["c_gfhb"][:], 1.0),
         reads=[], writes=["c_gfhb"])
    for dc in range(8):
        P.op("vector", lambda eng, dc=dc: eng.tensor_scalar(
            T["c_gfhb"][:, dc * 128:(dc + 1) * 128],
            T["c_gfhb"][:, dc * 128:(dc + 1) * 128],
            T["c_gh"][:, dc:dc + 1], None, ALU.mult),
            reads=["c_gfhb", "c_gh"], writes=["c_gfhb"])

    def load_light_weights():
        for v in ("lw1", "lw2"):
            P.arena_view(v, *FFN_VIEWS[v])
        for dlc in range(4):
            dma("gpsimd", "lw", T["lw1"][:, dlc, :, :], dr["w1l"][dlc],
                ["lw1"], ["lw1"])
            dma("gpsimd", "lw", T["lw2"][:, dlc, :], dr["w2l"][dlc],
                ["lw2"], ["lw2"])
        for dc in range(8):
            P.op("vector",
                 lambda eng, dc=dc: eng.tensor_scalar(
                     T["lw1"][:, :, dc, :], T["lw1"][:, :, dc, :],
                     T["c_gl"][:, dc:dc + 1], None, ALU.mult),
                 reads=["lw1", "c_gl"], writes=["lw1"])

    def ln_phase(src, srcpre, with_ffn):
        for c in range(dbg_ln_chunks):
            xs = T["xs0"] if c % 2 == 0 else T["xs1"]
            xsr = f"xs{c % 2}"
            lnb = T["ln0"]
            lnr = "ln0"
            dma("sync", f"xld{c % 2}", xs[:], src[c],
                [f"{srcpre}{c}"] if srcpre else [], [xsr])
            cc = slice(c, c + 1)
            P.op("scalar", lambda eng, xs=xs, cc=cc: eng.activation(
                T["tmpc"][:], xs[:], AF.Square, accum_out=T["ssq"][:, cc]),
                reads=[xsr], writes=["tmpc", f"ssq{c}"])
            P.op("scalar", lambda eng, xs=xs, lnb=lnb, cc=cc: eng.activation(
                lnb[:], xs[:], AF.Identity, accum_out=T["sumx"][:, cc]),
                reads=[xsr], writes=[lnr, f"sumx{c}"])
            P.op("vector", lambda eng, cc=cc: eng.tensor_scalar(
                T["mu"][:, cc], T["sumx"][:, cc], 1.0 / D, None, ALU.mult),
                reads=[f"sumx{c}"], writes=[f"mu{c}"])
            P.op("vector", lambda eng, cc=cc: eng.tensor_tensor(
                T["t16a"][:, cc], T["mu"][:, cc], T["mu"][:, cc], ALU.mult),
                reads=[f"mu{c}"], writes=[f"t16a{c}"])
            P.op("vector", lambda eng, cc=cc: eng.scalar_tensor_tensor(
                T["var"][:, cc], T["ssq"][:, cc], 1.0 / D, T["t16a"][:, cc],
                ALU.mult, ALU.subtract), reads=[f"ssq{c}", f"t16a{c}"], writes=[f"var{c}"])
            P.op("scalar", lambda eng, cc=cc: eng.activation(
                T["t16b"][:, cc], T["var"][:, cc], AF.Ln, bias=np.float32(EPS)),
                reads=[f"var{c}"], writes=[f"t16b{c}"])
            P.op("scalar", lambda eng, cc=cc: eng.activation(
                T["rstd"][:, cc], T["t16b"][:, cc], AF.Exp, scale=-0.5,
                accum_out=T["junk1"][:]), reads=[f"t16b{c}"],
                writes=[f"rstd{c}"])
            P.op("vector", lambda eng, xs=xs, lnb=lnb, cc=cc: eng.tensor_scalar(
                lnb[:], xs[:], T["mu"][:, cc], T["rstd"][:, cc],
                ALU.subtract, ALU.mult), reads=[xsr, f"mu{c}", f"rstd{c}"], writes=[lnr])
            if with_ffn:
                P.op("vector", lambda eng, lnb=lnb: eng.tensor_tensor(
                    T["tmpc"][:], lnb[:], T["c_rrep"][:], ALU.mult),
                    reads=[lnr, "c_rrep"], writes=["tmpc"])
                P.op("vector", lambda eng, cc=cc: eng.reduce_sum(
                    T["s16"][:, cc], T["tmpc"][:], AX),
                    reads=["tmpc"], writes=[f"s16_{c}"])
                P.op("vector", lambda eng, cc=cc: eng.tensor_tensor(
                    T["t16a"][:, cc], T["rstd"][:, cc], T["rstd"][:, cc],
                    ALU.mult), reads=[f"rstd{c}"], writes=[f"t16a{c}"])
                P.op("vector", lambda eng, cc=cc: eng.scalar_tensor_tensor(
                    T["t16b"][:, cc], T["var"][:, cc], float(D), T["t16a"][:, cc],
                    ALU.mult, ALU.mult), reads=[f"var{c}", f"t16a{c}"], writes=[f"t16b{c}"])
                P.op("vector", lambda eng, cc=cc: eng.tensor_scalar(
                    T["t16b"][:, cc], T["t16b"][:, cc], 1e-24, None, ALU.max),
                    reads=[f"t16b{c}"], writes=[f"t16b{c}"])
                P.op("scalar", lambda eng, cc=cc: eng.activation(
                    T["t16a"][:, cc], T["t16b"][:, cc], AF.Ln, scale=1.0 / 1024.0,
                    accum_out=T["junk1"][:]),
                    reads=[f"t16b{c}"], writes=[f"t16a{c}"])
                P.op("scalar", lambda eng, cc=cc: eng.activation(
                    T["tsc"][:, cc], T["t16a"][:, cc], AF.Exp, scale=-0.5,
                    accum_out=T["junk1"][:]),
                    reads=[f"t16a{c}"], writes=[f"tsc{c}"])
                P.op("vector", lambda eng, lnb=lnb, cc=cc: eng.tensor_scalar(
                    T["ys"][:], lnb[:], T["tsc"][:, cc], None, ALU.mult),
                    reads=[lnr, f"tsc{c}"], writes=["ys"])
            else:
                P.op("vector", lambda eng, lnb=lnb: eng.tensor_copy(
                    T["ys"][:], lnb[:]), reads=[lnr, "ys"], writes=["ys"])
            for g in range(2):
                pt, ptr = pool.alloc()
                for k in range(4):
                    dc = g * 4 + k
                    P.op("tensor", lambda eng, pt=pt, dc=dc, k=k: eng.matmul(
                        pt[:, k * 128:(k + 1) * 128],
                        T["ys"][:, dc * 128:(dc + 1) * 128], T["c_i128bf"][:],
                        start=True, stop=True),
                        reads=["ys", "c_i128bf", ptr], writes=[ptr])
                P.op("scalar", lambda eng, pt=pt, g=g, c=c: eng.copy(
                    T["act"][:, g * 4:(g + 1) * 4, c * 128:(c + 1) * 128],
                    pt[:].rearrange("p (a b) -> p a b", a=4)),
                    reads=[ptr], writes=[f"act{g}_{c // 4}"])

    def coor_mask_units():
        P.op("vector", lambda eng: eng.memset(T["sb16"][:], 0.0),
             reads=[f"s16_{c}" for c in range(NCH)], writes=["sb16"])
        P.op("vector", lambda eng: eng.memset(T["c1"][:], LOGK),
             reads=[f"s16_{c}" for c in range(NCH)], writes=["c1", "s16x"])
        yield
        for it in range(50):
            P.op("scalar", lambda eng: eng.activation(
                T["junk16"][:], T["sb16"][:], AF.Exp,
                accum_out=T["part1"][:]),
                reads=["sb16"], writes=["part1"])
            pt, ptr = pool.alloc()
            P.op("tensor", lambda eng, pt=pt: eng.matmul(
                pt[:, 0:1], T["c_ones128f"][:], T["part1"][:],
                start=True, stop=True),
                reads=["part1", "c_ones128f", ptr], writes=[ptr])
            P.op("scalar", lambda eng, pt=pt: eng.activation(
                T["l1"][:], pt[:, 0:1], AF.Ln), reads=[ptr], writes=["l1"])
            P.op("vector", lambda eng: eng.tensor_scalar(
                T["a1"][:], T["l1"][:], -1.0, T["c1"][:, 0:1], ALU.mult, ALU.add),
                reads=["l1", "c1"], writes=["a1"])
            P.op("vector", lambda eng: eng.tensor_scalar(
                T["c1"][:], T["a1"][:], LOGK, None, ALU.add),
                reads=["a1"], writes=["c1"])
            P.op("vector", lambda eng: eng.tensor_scalar(
                T["sb16"][:], T["s16"][:], T["a1"][:, 0:1], 0.0,
                ALU.add, ALU.min), reads=["s16x", "a1"], writes=["sb16"])
            yield
        P.op("vector", lambda eng: eng.tensor_scalar(
            T["u16"][:], T["s16"][:], T["a1"][:, 0:1], None, ALU.add),
            reads=["s16x", "a1"], writes=["u16"])
        P.op("vector", lambda eng: eng.tensor_scalar(
            T["t16a"][:], T["u16"][:], 0.0, None, ALU.max),
            reads=["u16"], writes=["t16a"])
        P.op("vector", lambda eng: eng.tensor_tensor(
            T["z16"][:], T["u16"][:], T["t16a"][:], ALU.subtract),
            reads=["u16", "t16a"], writes=["z16"])
        P.op("vector", lambda eng: eng.tensor_scalar(
            T["u01"][:], T["z16"][:], 0.0, None, ALU.is_equal),
            reads=["z16"], writes=["u01"])
        yield
        pt, ptr = pool.alloc()
        P.op("tensor", lambda eng, pt=pt: eng.matmul(
            pt[0:16, 0:128], T["z16"][:], T["c_i128f"][:],
            start=True, stop=True), reads=["z16", "c_i128f", ptr], writes=[ptr])
        P.op("vector", lambda eng, pt=pt: eng.tensor_copy(
            T["zt"][:], pt[0:16, 0:128]), reads=[ptr], writes=["zt"])
        for c in range(NCH):
            dma("sync", "zf", T["zflat"][:, c * 128:(c + 1) * 128],
                T["zt"][c:c + 1, :], ["zt"], [f"zflat{c}"])
        yield
        for j in range(4):
            pt2, ptr2 = pool.alloc()
            P.op("tensor", lambda eng, pt2=pt2, j=j: eng.matmul(
                pt2[:], T["c_ones1f"][:], T["zflat"][:, j * 512:(j + 1) * 512],
                start=True, stop=True),
                reads=[f"zflat{4 * j}", f"zflat{4 * j + 1}", f"zflat{4 * j + 2}",
                       f"zflat{4 * j + 3}", "c_ones1f", ptr2], writes=[ptr2])
            P.op("vector", lambda eng, pt2=pt2, j=j: eng.tensor_copy(
                T["zR"][:, j * 512:(j + 1) * 512], pt2[:]),
                reads=[ptr2, "zR"], writes=["zR"])
            yield
        for c in range(NCH):
            P.op("vector", lambda eng, c=c: eng.tensor_scalar(
                T["junk2k"][:], T["zR"][:], T["z16"][:, c:c + 1], None,
                ALU.is_gt, ALU.add, accum_out=T["cg16"][:, c:c + 1]),
                reads=["zR", "z16"], writes=["cg16"])
            if c % 4 == 3:
                yield
        pt3, ptr3 = pool.alloc()
        P.op("tensor", lambda eng, pt3=pt3: eng.matmul(
            pt3[:, 0:16], T["c_l128f"][:], T["u01"][:], start=True, stop=True),
            reads=["u01", "c_l128f", ptr3], writes=[ptr3])
        P.op("vector", lambda eng, pt3=pt3: eng.tensor_copy(
            T["t16a"][:], pt3[:, 0:16]), reads=[ptr3], writes=["t16a"])
        yield
        pt4, ptr4 = pool.alloc()
        P.op("tensor", lambda eng, pt4=pt4: eng.matmul(
            pt4[0:16, 0:1], T["u01"][:], T["c_ones128f"][:, 0:1],
            start=True, stop=True), reads=["u01", "c_ones128f", ptr4], writes=[ptr4])
        P.op("vector", lambda eng, pt4=pt4: eng.tensor_copy(
            T["t16c"][:], pt4[0:16, 0:1]), reads=[ptr4], writes=["t16c"])
        pt5, ptr5 = pool.alloc()
        P.op("tensor", lambda eng, pt5=pt5: eng.matmul(
            pt5[0:16, 0:1], T["c_l16f"][:], T["t16c"][:], start=True, stop=True),
            reads=["t16c", "c_l16f", ptr5], writes=[ptr5])
        P.op("vector", lambda eng, pt5=pt5: eng.tensor_copy(
            T["t16c"][:], pt5[0:16, 0:1]), reads=[ptr5, "t16c"], writes=["t16c"])
        yield
        pt6, ptr6 = pool.alloc()
        P.op("tensor", lambda eng, pt6=pt6: eng.matmul(
            pt6[0:1, 0:16], T["t16c"][:], T["c_i16f"][:], start=True, stop=True),
            reads=["t16c", "c_i16f", ptr6], writes=[ptr6])
        P.op("vector", lambda eng, pt6=pt6: eng.tensor_copy(
            T["offs"][:], pt6[0:1, 0:16]), reads=[ptr6], writes=["offs"])
        pt7, ptr7 = pool.alloc()
        P.op("tensor", lambda eng, pt7=pt7: eng.matmul(
            pt7[:, 0:16], T["c_ones1f"][:], T["offs"][:], start=True, stop=True),
            reads=["offs", "c_ones1f", ptr7], writes=[ptr7])
        P.op("vector", lambda eng, pt7=pt7: eng.tensor_tensor(
            T["ci16"][:], T["t16a"][:], pt7[:, 0:16], ALU.add),
            reads=["t16a", ptr7], writes=["ci16"])
        P.op("vector", lambda eng: eng.tensor_tensor(
            T["t16b"][:], T["u01"][:], T["ci16"][:], ALU.mult),
            reads=["u01", "ci16"], writes=["t16b"])
        P.op("vector", lambda eng: eng.tensor_tensor(
            T["t16a"][:], T["t16b"][:], T["cg16"][:], ALU.add),
            reads=["t16b", "cg16"], writes=["t16a"])
        P.op("vector", lambda eng: eng.tensor_scalar(
            T["mask16"][:], T["t16a"][:], 1024.0, None, ALU.is_lt),
            reads=["t16a"], writes=["mask16"])

    def ffn_compute(srcdram, srcpre, dstdram, dstpre, coor=None, convert=False):
        def drain(n):
            if coor is not None:
                for _ in range(n):
                    if next(coor, "END") == "END":
                        break
        for q in range(4):
            for dhc in range(32):
                slot = dhc % 2
                if convert and q == 0:
                    xs = T["xs0"] if slot == 0 else T["xs1"]
                    if dhc < 2:
                        dma("sync", f"wld{slot}", xs[:],
                            dr["w1h"][dhc].rearrange("p a b -> p (a b)"),
                            [], [f"xs{slot}"])
                    P.op("vector", lambda eng, xs=xs, s=slot: eng.tensor_copy(
                        T["w1s"][:, s, :], xs[:]),
                        reads=[f"xs{slot}"], writes=[f"w1s{slot}"])
                    if dhc + 2 < 32:
                        dma("sync", f"wld{slot}", xs[:],
                            dr["w1h"][dhc + 2].rearrange("p a b -> p (a b)"),
                            [], [f"xs{slot}"])
                    P.op("vector", lambda eng, s=slot: eng.tensor_tensor(
                        T["w1s"][:, s, :], T["w1s"][:, s, :], T["c_gfhb"][:],
                        ALU.mult),
                        reads=[f"w1s{slot}", "c_gfhb"], writes=[f"w1s{slot}"])
                else:
                    dma("sync", f"w1f{slot}", T["w1s"][:, slot, :], w1hbf_dr[dhc],
                        [f"w1hbf{dhc}"], [f"w1s{slot}"])
                pt, ptr = pool.alloc()
                for dc in range(8):
                    P.op("tensor", lambda eng, pt=pt, s=slot, dc=dc, q=q: eng.matmul(
                        pt[:], T["w1s"][:, s, dc * 128:(dc + 1) * 128],
                        T["act"][:, dc, q * 512:(q + 1) * 512],
                        start=(dc == 0), stop=(dc == 7)),
                        reads=[f"w1s{slot}", f"act{dc // 4}_{q}", ptr], writes=[ptr])
                P.op("scalar", lambda eng, pt=pt, dhc=dhc: eng.activation(
                    T["h1"][:, dhc, :], pt[:], AF.Gelu,
                    bias=T["c_b1h"][:, dhc:dhc + 1]),
                    reads=[ptr, "c_b1h"], writes=[f"h1_{dhc}"])
                if convert and q == 0:
                    dma("gpsimd", f"wst{slot}", w1hbf_dr[dhc], T["w1s"][:, slot, :],
                        [f"w1s{slot}"], [f"w1hbf{dhc}"])
                if q == 0:
                    drain(2)
            for dlc in range(4):
                pt, ptr = pool.alloc()
                for dc in range(8):
                    P.op("tensor", lambda eng, pt=pt, dlc=dlc, dc=dc, q=q: eng.matmul(
                        pt[:], T["lw1"][:, dlc, dc, :],
                        T["act"][:, dc, q * 512:(q + 1) * 512],
                        start=(dc == 0), stop=(dc == 7)),
                        reads=["lw1", f"act{dc // 4}_{q}", ptr], writes=[ptr])
                P.op("scalar", lambda eng, pt=pt, dlc=dlc: eng.activation(
                    T["h1l"][:, dlc, :], pt[:], AF.Gelu,
                    bias=T["c_b1l"][:, dlc:dlc + 1]),
                    reads=[ptr, "c_b1l"], writes=[f"h1l_{dlc}"])
                if q == 0:
                    drain(2)
            for ncq in range(4):
                for hd in range(2):
                    pt, ptr = pool.alloc()
                    for dlc in range(4):
                        P.op("tensor", lambda eng, pt=pt, dlc=dlc, ncq=ncq, hd=hd: eng.matmul(
                            pt[:], T["h1l"][:, dlc, ncq * 128:(ncq + 1) * 128],
                            T["lw2"][:, dlc, hd * 512:(hd + 1) * 512],
                            start=(dlc == 0), stop=False),
                            reads=[f"h1l_{dlc}", "lw2", ptr], writes=[ptr])
                    P.op("tensor", lambda eng, pt=pt, hd=hd: eng.matmul(
                        pt[:], T["c_ones1bf"][:], T["c_b2l"][:, hd * 512:(hd + 1) * 512],
                        start=False, stop=True),
                        reads=["c_ones1bf", "c_b2l", ptr], writes=[ptr])
                    P.op("vector", lambda eng, pt=pt, ncq=ncq, hd=hd: eng.tensor_copy(
                        T["lout"][:, ncq, hd * 512:(hd + 1) * 512], pt[:]),
                        reads=[ptr], writes=[f"lout{ncq}_{hd}"])
                    if q == 0:
                        drain(2)
            if q == 0:
                drain(10 ** 6)
            for ncq in (0, 1):
                c = q * 4 + ncq
                xr = T["xr0"] if c % 2 == 0 else T["xr1"]
                dma("sync", f"xrs{c % 2}", xr[:], srcdram[c],
                    [f"{srcpre}{c}"] if srcpre else [], [f"xr{c % 2}"])
            for dhc in range(32):
                slot = dhc % 2
                if convert and q == 0:
                    xs = T["xs0"] if slot == 0 else T["xs1"]
                    if dhc < 2:
                        dma("sync", f"wld{slot}", xs[:], dr["w2h"][dhc],
                            [], [f"xs{slot}"])
                    P.op("vector", lambda eng, xs=xs, s=slot: eng.tensor_copy(
                        T["w2s"][:, s, :], xs[:]),
                        reads=[f"xs{slot}"], writes=[f"w2s{slot}"])
                    if dhc + 2 < 32:
                        dma("sync", f"wld{slot}", xs[:], dr["w2h"][dhc + 2],
                            [], [f"xs{slot}"])
                else:
                    dma("gpsimd", f"w2f{slot}", T["w2s"][:, slot, :], w2hbf_dr[dhc],
                        [f"w2hbf{dhc}"], [f"w2s{slot}"])
                for ncq in range(4):
                    for hd in range(2):
                        b = ncq * 2 + hd
                        P.op("tensor", lambda eng, b=b, dhc=dhc, s=slot, ncq=ncq, hd=hd: eng.matmul(
                            pband[b][:], T["h1"][:, dhc, ncq * 128:(ncq + 1) * 128],
                            T["w2s"][:, s, hd * 512:(hd + 1) * 512],
                            start=(dhc == 0), stop=False),
                            reads=[f"h1_{dhc}", f"w2s{slot}", f"ps{b}"], writes=[f"ps{b}"])
                if convert and q == 0:
                    dma("gpsimd", f"wst{slot}", w2hbf_dr[dhc], T["w2s"][:, slot, :],
                        [f"w2s{slot}"], [f"w2hbf{dhc}"])
            for ncq in range(4):
                for hd in range(2):
                    b = ncq * 2 + hd
                    P.op("tensor", lambda eng, b=b, hd=hd: eng.matmul(
                        pband[b][:], T["c_ones1bf"][:],
                        T["c_b2h"][:, hd * 512:(hd + 1) * 512],
                        start=False, stop=True),
                        reads=["c_ones1bf", "c_b2h", f"ps{b}"], writes=[f"ps{b}"])
            for ncq in range(4):
                c = q * 4 + ncq
                xr = T["xr0"] if c % 2 == 0 else T["xr1"]
                xrr = f"xr{c % 2}"
                ou = T["ou0"] if c % 2 == 0 else T["ou1"]
                our = f"ou{c % 2}"
                for hd in range(2):
                    b = ncq * 2 + hd
                    sl = slice(hd * 512, (hd + 1) * 512)
                    P.op("vector", lambda eng, b=b, ncq=ncq, sl=sl, c=c: eng.scalar_tensor_tensor(
                        T["tmpc"][:, sl], pband[b][:], T["mask16"][:, c:c + 1],
                        T["lout"][:, ncq, sl], ALU.mult, ALU.add),
                        reads=[f"ps{b}", "mask16", f"lout{ncq}_{hd}"], writes=["tmpc"])
                    P.op("vector", lambda eng, xr=xr, ou=ou, sl=sl: eng.scalar_tensor_tensor(
                        ou[:, sl], T["tmpc"][:, sl], 0.5, xr[:, sl],
                        ALU.mult, ALU.add),
                        reads=["tmpc", xrr, our], writes=[our])
                dma("sync", f"st{c % 2}", dstdram[c], ou[:], [our], [f"{dstpre}{c}"])
                if ncq < 2 and q * 4 + ncq + 2 < 16:
                    c2 = q * 4 + ncq + 2
                    xr2 = T["xr0"] if c2 % 2 == 0 else T["xr1"]
                    dma("sync", f"xrs{c2 % 2}", xr2[:], srcdram[c2],
                        [f"{srcpre}{c2}"] if srcpre else [], [f"xr{c2 % 2}"])

    def ffn_views():
        for v, (o, s) in FFN_VIEWS.items():
            if v == "lout":
                for ncq in range(4):
                    for hd in range(2):
                        P.arena_view(f"lout{ncq}_{hd}", o, s)
            elif v == "h1":
                for dhc in range(32):
                    P.arena_view(f"h1_{dhc}", o, s)
            elif v == "h1l":
                for dlc in range(4):
                    P.arena_view(f"h1l_{dlc}", o, s)
            else:
                P.arena_view(v, o, s)
        for v, (o, s) in F32_VIEWS.items():
            P.arena_view(v, o, s)
        for c in range(NCH):
            P.arena_view(f"zflat{c}", ZFLAT_VIEW[0], ZFLAT_VIEW[1])
        dma("sync", "cst", T["c_rrep"][:], dr["rrep"], [], ["c_rrep"])

    def ffn_phase(srcdram, srcpre, dstdram, dstpre, convert=False):
        ffn_views()
        load_light_weights()
        ln_phase(srcdram, srcpre, True)
        if DBG and DBG_FFN:
            dump(T["ssq"][:], ["ssq"], 16)
            dump(T["var"][:], ["var"], 16)
            dump(T["rstd"][:], ["rstd"], 16)
            dump(T["tsc"][:], ["tsc"], 16)
            dump(T["ln0"][:], ["ln0"], 1024)
            dump(T["ys"][:], ["ys"], 1024)
            for dc in range(8):
                dump(T["act"][:, dc, :], [f"act{dc // 4}_{r}" for r in range(4)], 2048)
            dump(T["s16"][:], ["s16"], 16)
        coor = coor_mask_units()
        if DBG and DBG_FFN:
            for _ in coor:
                pass
            coor = None
            dump(T["a1"][:], ["a1"], 1)
            dump(T["z16"][:], ["z16"], 16)
            dump(T["u01"][:], ["u01"], 16)
            dump(T["cg16"][:], ["cg16"], 16)
            dump(T["ci16"][:], ["ci16"], 16)
            dump(T["mask16"][:], ["mask16"], 16)
            dump(T["zR"][:], ["zR"], 2048)
        ffn_compute(srcdram, srcpre, dstdram, dstpre, coor, convert)
        if DBG and DBG_FFN:
            dump(T["h1"][:, 0, :], ["h1_0"], 512)
            dump(T["h1l"][:, 0, :], ["h1l_0"], 512)
            dump(T["lout"][:, 0, :], ["lout0_0", "lout0_1"], 1024)

    def attn_phase():
        for v, (o, s) in ATTN_VIEWS.items():
            if v == "K":
                for nb in range(4):
                    P.arena_view(f"K{nb}", o, s)
            elif v == "wkv":
                for dc in range(8):
                    P.arena_view(f"wkv{dc}", o, s)
            else:
                P.arena_view(v, o, s)
        ln_phase(x1_dr, "x1:", False)
        for h in range(H):
            for dc in range(8):
                dma("gpsimd", "wkqv", T["wkv"][:, dc, :], dr["wkqv"][h, dc],
                    [], [f"wkv{dc}"])
            for ec in range(8):
                for nb in range(4):
                    pt, ptr = pool.alloc()
                    for dc in range(8):
                        P.op("tensor", lambda eng, pt=pt, dc=dc, ec=ec, nb=nb: eng.matmul(
                            pt[:], T["wkv"][:, dc, ec * 128:(ec + 1) * 128],
                            T["act"][:, dc, nb * 512:(nb + 1) * 512],
                            start=(dc == 0), stop=(dc == 7)),
                            reads=[f"wkv{dc}", f"act{dc // 4}_{nb}", ptr], writes=[ptr])
                    P.op("scalar", lambda eng, pt=pt, ec=ec, nb=nb, h=h: eng.add(
                        T["K"][:, ec, nb * 512:(nb + 1) * 512], pt[:],
                        T["c_bkq"][:, h * 8 + ec:h * 8 + ec + 1]),
                        reads=[ptr, "c_bkq"], writes=[f"K{ec}_{nb}"])
            for g in range(2):
                pt, ptr = pool.alloc()
                for j in range(8):
                    nc_ = g * 8 + j
                    dcc = nc_ // 2
                    for dc in range(8):
                        P.op("tensor", lambda eng, pt=pt, j=j, nc_=nc_, dc=dc: eng.matmul(
                            pt[:, j * 64:(j + 1) * 64],
                            T["act"][:, dc, nc_ * 128:(nc_ + 1) * 128],
                            T["wkv"][:, dc, 2048:2112],
                            start=(dc == 0), stop=False),
                            reads=["act%d_%d" % (dc // 4, nc_ // 4), f"wkv{dc}", ptr], writes=[ptr])
                    P.op("tensor", lambda eng, pt=pt, j=j, h=h: eng.matmul(
                        pt[:, j * 64:(j + 1) * 64], T["c_ones1bf"][:],
                        T["c_bv"][:, h * 64:(h + 1) * 64],
                        start=False, stop=True),
                        reads=["c_ones1bf", "c_bv", ptr], writes=[ptr])
                P.op("scalar", lambda eng, pt=pt, g=g: eng.copy(
                    T["V"][:, g * 8:(g + 1) * 8, :],
                    pt[:].rearrange("p (a b) -> p a b", a=8)),
                    reads=[ptr], writes=[f"V{g}"])
            deferred = [None]
            def flush():
                if deferred[0] is not None:
                    deferred[0]()
                    deferred[0] = None
            for qi in range(16):
                if qi % 4 == 0:
                    qq = qi // 4
                    for ec in range(8):
                        pt, ptr = pool.alloc()
                        for dc in range(8):
                            P.op("tensor", lambda eng, pt=pt, dc=dc, ec=ec, qq=qq: eng.matmul(
                                pt[:], T["wkv"][:, dc, 1024 + ec * 128:1024 + (ec + 1) * 128],
                                T["act"][:, dc, qq * 512:(qq + 1) * 512],
                                start=(dc == 0), stop=(dc == 7)),
                                reads=[f"wkv{dc}", f"act{dc // 4}_{qq}", ptr], writes=[ptr])
                        P.op("vector", lambda eng, pt=pt, ec=ec, h=h: eng.tensor_scalar(
                            T["Q"][:, ec, :], pt[:],
                            T["c_bkq"][:, 128 + h * 8 + ec:128 + h * 8 + ec + 1],
                            0.03125, ALU.add, ALU.mult),
                            reads=[ptr, "c_bkq"], writes=[f"Q{ec}"])
                kbs = qi // 4 + 1
                passes = 1 if kbs <= 2 else 2
                ltiles = []
                for kb in range(kbs):
                    pt, ptr = pool.alloc()
                    ltiles.append((pt, ptr))
                    last = kb == kbs - 1
                    for ec in range(8):
                        P.op("tensor", lambda eng, pt=pt, ec=ec, kb=kb, qi=qi, last=last: eng.matmul(
                            pt[:], T["Q"][:, ec, (qi % 4) * 128:(qi % 4 + 1) * 128],
                            T["K"][:, ec, kb * 512:(kb + 1) * 512],
                            start=(ec == 0), stop=(ec == 7)),
                            reads=[f"Q{ec}", f"K{ec}_{kb}", ptr], writes=[ptr])
                    if last:
                        j = qi % 4
                        P.op("vector", lambda eng, pt=pt, j=j: eng.scalar_tensor_tensor(
                            pt[:], T["c_cm"][:, 384 - j * 128:896 - j * 128],
                            NEG, pt[:], ALU.mult, ALU.add),
                            reads=["c_cm", ptr], writes=[ptr])
                flush()
                for kb, (pt, ptr) in enumerate(ltiles):
                    P.op("vector", lambda eng, pt=pt, kb=kb: eng.reduce_max(
                        T["mxp"][:, kb:kb + 1], pt[:], AX, negate=True),
                        reads=[ptr], writes=[f"mxp{kb}"])
                pb = [(0, min(kbs, 2))] if passes == 1 else [(0, 2), (2, kbs)]
                for pi, (k0, k1) in enumerate(pb):
                    mb = T["m0"] if pi == 0 else T["m1"]
                    mbr = "m0" if pi == 0 else "m1"
                    if k1 - k0 == 2:
                        P.op("vector", lambda eng, mb=mb, k0=k0: eng.tensor_tensor(
                            mb[:], T["mxp"][:, k0:k0 + 1], T["mxp"][:, k0 + 1:k0 + 2],
                            ALU.min), reads=[f"mxp{k0}", f"mxp{k0 + 1}"], writes=[mbr])
                    else:
                        P.op("vector", lambda eng, mb=mb, k0=k0: eng.tensor_copy(
                            mb[:], T["mxp"][:, k0:k0 + 1]), reads=[f"mxp{k0}"], writes=[mbr])
                    for kb in range(k0, k1):
                        pt, ptr = ltiles[kb]
                        P.op("scalar", lambda eng, pt=pt, kb=kb, mb=mb, mbr=mbr: eng.activation(
                            T["Pb"][:, kb * 512:(kb + 1) * 512], pt[:], AF.Exp,
                            bias=mb[:, 0:1],
                            accum_out=T[f"sp{kb}"][:]),
                            reads=[ptr, mbr], writes=[f"Pb{kb}", f"sp{kb}"])
                if passes == 1:
                    if kbs == 2:
                        P.op("vector", lambda eng: eng.tensor_tensor(
                            T["S1"][:], T["sp0"][:], T["sp1"][:], ALU.add),
                            reads=["sp0", "sp1"], writes=["S1"])
                    else:
                        P.op("vector", lambda eng: eng.tensor_copy(
                            T["S1"][:], T["sp0"][:]), reads=["sp0"], writes=["S1"])
                    P.op("vector", lambda eng: eng.reciprocal(
                        T["rS"][:], T["S1"][:]), reads=["S1"], writes=["rS"])
                    P.op("vector", lambda eng: eng.tensor_scalar(
                        T["db0"][:], T["c_i128bf"][:], T["rS"][:, 0:1], None,
                        ALU.mult), reads=["rS", "c_i128bf", "db0"], writes=["db0"])
                else:
                    P.op("vector", lambda eng: eng.tensor_tensor(
                        T["negm"][:], T["m0"][:], T["m1"][:], ALU.min),
                        reads=["m0", "m1"], writes=["negm"])
                    P.op("scalar", lambda eng: eng.activation(
                        T["cp0"][:], T["m0"][:], AF.Exp, bias=T["negm"][:, 0:1],
                        scale=-1.0, accum_out=T["junk1"][:]),
                        reads=["m0", "negm"], writes=["cp0"])
                    P.op("scalar", lambda eng: eng.activation(
                        T["cp1"][:], T["m1"][:], AF.Exp, bias=T["negm"][:, 0:1],
                        scale=-1.0, accum_out=T["junk1"][:]),
                        reads=["m1", "negm"], writes=["cp1"])
                    P.op("vector", lambda eng: eng.tensor_tensor(
                        T["t16a"][:, 0:1], T["sp0"][:], T["sp1"][:], ALU.add),
                        reads=["sp0", "sp1"], writes=["t16a"])
                    if kbs == 4:
                        P.op("vector", lambda eng: eng.tensor_tensor(
                            T["t16b"][:, 0:1], T["sp2"][:], T["sp3"][:], ALU.add),
                            reads=["sp2", "sp3"], writes=["t16b"])
                    else:
                        P.op("vector", lambda eng: eng.tensor_copy(
                            T["t16b"][:, 0:1], T["sp2"][:]),
                            reads=["sp2"], writes=["t16b"])
                    P.op("vector", lambda eng: eng.tensor_tensor(
                        T["t16a"][:, 0:1], T["t16a"][:, 0:1], T["cp0"][:], ALU.mult),
                        reads=["t16a", "cp0"], writes=["t16a"])
                    P.op("vector", lambda eng: eng.scalar_tensor_tensor(
                        T["S1"][:], T["t16b"][:, 0:1], 1.0, T["cp1"][:],
                        ALU.mult, ALU.mult), reads=["t16b", "cp1"], writes=["S1"])
                    P.op("vector", lambda eng: eng.tensor_tensor(
                        T["S1"][:], T["S1"][:], T["t16a"][:, 0:1], ALU.add),
                        reads=["S1", "t16a"], writes=["S1"])
                    P.op("vector", lambda eng: eng.reciprocal(
                        T["rS"][:], T["S1"][:]), reads=["S1"], writes=["rS"])
                    P.op("vector", lambda eng: eng.tensor_tensor(
                        T["d0"][:], T["cp0"][:], T["rS"][:], ALU.mult),
                        reads=["cp0", "rS"], writes=["d0"])
                    P.op("vector", lambda eng: eng.tensor_tensor(
                        T["d1"][:], T["cp1"][:], T["rS"][:], ALU.mult),
                        reads=["cp1", "rS"], writes=["d1"])
                    P.op("vector", lambda eng: eng.tensor_scalar(
                        T["db0"][:], T["c_i128bf"][:], T["d0"][:, 0:1], None,
                        ALU.mult), reads=["d0", "c_i128bf", "db0"], writes=["db0"])
                    P.op("vector", lambda eng: eng.tensor_scalar(
                        T["db1"][:], T["c_i128bf"][:], T["d1"][:, 0:1], None,
                        ALU.mult), reads=["d1", "c_i128bf", "db1"], writes=["db1"])

                def make_ptav(qi=qi, h=h):
                    def emit_ptav():
                        nkc = qi + 1
                        for kcg in range((nkc + 3) // 4):
                            pt, ptr = ptpool.alloc()
                            k0 = kcg * 4
                            kn = min(4, nkc - k0)
                            for j in range(kn):
                                kc = k0 + j
                                db = "db0" if kc < 8 else "db1"
                                P.op("tensor", lambda eng, pt=pt, j=j, kc=kc, db=db: eng.matmul(
                                    pt[:, j * 128:(j + 1) * 128],
                                    T["Pb"][:, kc * 128:(kc + 1) * 128], T[db][:],
                                    start=True, stop=True),
                                    reads=[f"Pb{kc // 4}", db, ptr], writes=[ptr])
                            P.op("vector", lambda eng, pt=pt, k0=k0, kn=kn: eng.tensor_copy(
                                T["PT"][:, k0 * 128:(k0 + kn) * 128],
                                pt[:, 0:kn * 128]), reads=[ptr], writes=[f"PT{k0 // 4}"])
                        for kc in range(nkc):
                            P.op("tensor", lambda eng, kc=kc, qi=qi: eng.matmul(
                                pband[7][0:64, (qi % 4) * 128:(qi % 4 + 1) * 128],
                                T["V"][:, kc, :], T["PT"][:, kc * 128:(kc + 1) * 128],
                                start=(kc == 0), stop=(kc == qi)),
                                reads=[f"V{kc // 8}", f"PT{kc // 4}", "ps7"], writes=["ps7"])
                        if qi % 4 == 3:
                            po = (h % 2) * 64
                            P.op("scalar", lambda eng, qi=qi, h=h, po=po: eng.copy(
                                T["sa"][po:po + 64, h // 2, (qi - 3) * 128:(qi + 1) * 128],
                                pband[7][0:64, :]), reads=["ps7", "sa"], writes=["sa"])
                    return emit_ptav
                if deferred[0] is not None:
                    flush()
                deferred[0] = make_ptav()
            flush()
        P.arena_view("wpb", *WP_VIEW["wpb"])
        for dc in range(8):
            dma("gpsimd", "wp", T["wpb"][:, dc, :], dr["wp"][dc], [], ["wpb"])
        for c in range(NCH):
            xr = T["xr0"] if c % 2 == 0 else T["xr1"]
            xrr = f"xr{c % 2}"
            dma("sync", f"xrs{c % 2}", xr[:], x1_dr[c], [f"x1:{c}"], [xrr])
            ou = T["ou0"] if c % 2 == 0 else T["ou1"]
            our = f"ou{c % 2}"
            for hd in range(2):
                pt, ptr = pool.alloc()
                for dc in range(8):
                    P.op("tensor", lambda eng, pt=pt, dc=dc, c=c, hd=hd: eng.matmul(
                        pt[:], T["sa"][:, dc, c * 128:(c + 1) * 128],
                        T["wpb"][:, dc, hd * 512:(hd + 1) * 512],
                        start=(dc == 0), stop=False),
                        reads=["sa", "wpb", ptr], writes=[ptr])
                P.op("tensor", lambda eng, pt=pt, hd=hd: eng.matmul(
                    pt[:], T["c_ones1bf"][:], T["c_bp"][:, hd * 512:(hd + 1) * 512],
                    start=False, stop=True),
                    reads=["c_ones1bf", "c_bp", ptr], writes=[ptr])
                P.op("vector", lambda eng, pt=pt, ou=ou, hd=hd, xr=xr: eng.tensor_tensor(
                    ou[:, hd * 512:(hd + 1) * 512], pt[:],
                    xr[:, hd * 512:(hd + 1) * 512], ALU.add),
                    reads=[ptr, xrr, our], writes=[our])
            dma("sync", f"st{c % 2}", x2_dr[c], ou[:], [our], [f"x2:{c}"])

    alloc("mxp", [128, 4])
    if phases == ("cvt",):
        for c in range(NCH):
            dma("sync", "dbg", out_dr[c], dr["x0"][c], [], [f"out:{c}"])
    elif phases == ("lnonly",):
        ffn_views()
        if "light" not in dbg_skip:
            load_light_weights()
        ln_phase(dr["x0"], "", "s" not in dbg_skip)
        for c in range(NCH):
            dma("sync", "dbg", out_dr[c], dr["x0"][c], [], [f"out:{c}"])
    elif phases == ("ffn1",):
        ffn_phase(dr["x0"], "", out_dr, "out:", convert=True)
    elif phases == ("ffn1", "attn"):
        ffn_phase(dr["x0"], "", x1_dr, "x1:", convert=True)
        attn_phase()
        for c in range(NCH):
            dma("sync", "dbg", out_dr[c], x2_dr[c], [f"x2:{c}"], [f"out:{c}"])
    else:
        ffn_phase(dr["x0"], "", x1_dr, "x1:", convert=True)
        attn_phase()
        ffn_phase(x2_dr, "x2:", out_dr, "out:")

    P.finalize(nc)
    return nc



def _host_prep(inputs):
    f32 = np.float32
    g = lambda k: np.ascontiguousarray(np.asarray(inputs[k], f32))
    Wkqv = g("L0_attn_Wkqv")
    bkqv = g("L0_attn_bkqv")
    d = {}
    d["wkqv"] = np.ascontiguousarray(Wkqv.reshape(H, 8, 128, 2112))
    d["wp"] = np.ascontiguousarray(g("L0_attn_Wp").reshape(8, 128, D))
    d["bkq"] = np.ascontiguousarray(np.concatenate(
        [bkqv[:, 0:1024].reshape(H, 8, 128).transpose(2, 0, 1).reshape(128, 128),
         bkqv[:, 1024:2048].reshape(H, 8, 128).transpose(2, 0, 1).reshape(128, 128)],
        axis=1))
    d["bv"] = np.ascontiguousarray(bkqv[:, 2048:2112].reshape(1, H * DH))
    d["bp"] = np.ascontiguousarray(g("L0_attn_bp").reshape(1, D))
    d["w1l"] = np.ascontiguousarray(
        g("L0_ffn_w1_l").T.reshape(8, 128, 4, 128).transpose(2, 1, 0, 3))
    d["w2l"] = np.ascontiguousarray(g("L0_ffn_w2_l").T.reshape(4, 128, D))
    d["w1h"] = np.ascontiguousarray(
        g("L0_ffn_w1_h").T.reshape(8, 128, 32, 128).transpose(2, 1, 0, 3))
    d["w2h"] = np.ascontiguousarray(g("L0_ffn_w2_h").T.reshape(32, 128, D))
    d["b1h"] = np.ascontiguousarray(g("L0_ffn_b1_h").reshape(32, 128).T)
    d["b1l"] = np.ascontiguousarray(g("L0_ffn_b1_l").reshape(4, 128).T)
    d["b2l"] = np.ascontiguousarray(g("L0_ffn_b2_l").reshape(1, D))
    d["b2h"] = np.ascontiguousarray(g("L0_ffn_b2_h").reshape(1, D))
    d["gl"] = np.ascontiguousarray(g("L0_ffn_gamma_l").reshape(8, 128).T)
    d["gh"] = np.ascontiguousarray(g("L0_ffn_gamma_h").reshape(8, 128).T)
    d["rrep"] = np.ascontiguousarray(np.broadcast_to(
        g("L0_ffn_routing_token").reshape(1, D), (128, D)))
    cm = np.zeros((128, 896), f32)
    mm_ = np.arange(896)
    for p in range(128):
        cm[p, mm_ > p + 384] = 1.0
    d["cmask"] = cm.astype(BFNP)
    d["dneg"] = (np.eye(128, dtype=f32) * -65536.0).astype(BFNP)
    d["i128bf"] = np.eye(128, dtype=BFNP)
    d["i128f"] = np.eye(128, dtype=f32)
    d["ones1bf"] = np.ones((1, 128), BFNP)
    d["ones1f"] = np.ones((1, 128), np.float32)
    d["ones128f"] = np.ones((128, 128), f32)
    d["l128f"] = np.triu(np.ones((128, 128), f32), 1)
    d["i16f"] = np.eye(16, dtype=f32)
    d["l16f"] = np.triu(np.ones((16, 16), f32), 1)
    return d


def kernel(**inputs):
    host = _host_prep(inputs)
    xfull = np.ascontiguousarray(np.asarray(inputs["L0_attn_x"], np.float32))
    nc = build_nc()
    in_maps = []
    for b in range(NCORES):
        m = dict(host)
        m["x0"] = np.ascontiguousarray(xfull[b].reshape(NCH, 128, D))
        in_maps.append(m)
    res = run_bass_kernel_spmd(nc, in_maps, core_ids=list(range(NCORES)))
    out = np.stack([np.asarray(res.results[b]["out"]).reshape(N, D)
                    for b in range(NCORES)])
    return out


# version 14
# speedup vs baseline: 1.0287x; 1.0287x over previous
import contextlib
import math
import bisect
import numpy as np
import ml_dtypes
import concourse.bass as bass
import concourse.mybir as mybir
from concourse.bass_utils import run_bass_kernel_spmd

F32 = mybir.dt.float32
BF16 = mybir.dt.bfloat16
AX = mybir.AxisListType.X
AF = mybir.ActivationFunctionType
ALU = mybir.AluOpType
BFNP = ml_dtypes.bfloat16

NCORES = 8
N = 2048
D = 1024
H = 16
DH = 64
DHEAVY = 4096
DLIGHT = 512
NCH = 16
EPS = 1e-5
KEFF = 1024 * 9.0 / 8.0
LOGK = float(np.log(np.float32(KEFF)))
LN32 = float(np.log(np.float32(32.0)))
NEG = -1.0e30

ENGINES = ("tensor", "vector", "scalar", "sync", "gpsimd")
DBG = False
DBG_FFN = False


class _Res:
    __slots__ = ("writes", "reads")

    def __init__(self):
        self.writes = {}
        self.reads = {}


class _Op:
    __slots__ = ("emit", "waits", "tok", "seq")

    def __init__(self, emit, waits, tok, seq):
        self.emit = emit
        self.waits = waits
        self.tok = tok
        self.seq = seq


class Prog:

    def __init__(self):
        self.ops = {e: [] for e in ENGINES}
        self.seq = {e: 0 for e in ENGINES}
        self.res = {}
        self.know = {e: {} for e in ENGINES}
        self.snap = {}
        self.dma_count = {}
        self.arena = []

    def _dep_tokens(self, reads, writes):
        deps = {}
        for r in reads:
            st = self.res.get(r)
            if st:
                for k, v in st.writes.items():
                    if deps.get(k, 0) < v:
                        deps[k] = v
        for w in writes:
            st = self.res.get(w)
            if st:
                for k, v in st.writes.items():
                    if deps.get(k, 0) < v:
                        deps[k] = v
                for k, v in st.reads.items():
                    if deps.get(k, 0) < v:
                        deps[k] = v
        return deps

    def op(self, e, emit, reads=(), writes=(), dma=None):
        deps = self._dep_tokens(reads, writes)
        know = self.know[e]
        waits = []
        for k, v in deps.items():
            if k == ("E", e) and e == "tensor":
                continue
            if know.get(k, 0) >= v:
                continue
            waits.append((k, v))
            know[k] = v
            sn = self.snap.get((k, v))
            if sn:
                for k2, v2 in sn.items():
                    if know.get(k2, 0) < v2:
                        know[k2] = v2
        if dma is None:
            self.seq[e] += 1
            tok = (("E", e), self.seq[e])
            seq = self.seq[e]
        else:
            prev = self.dma_count.get(dma, 0)
            if prev and know.get(("D", dma), 0) < prev:
                waits.append((("D", dma), prev))
                know[("D", dma)] = prev
                sn = self.snap.get((("D", dma), prev))
                if sn:
                    for k2, v2 in sn.items():
                        if know.get(k2, 0) < v2:
                            know[k2] = v2
            c = prev + 16
            self.dma_count[dma] = c
            tok = (("D", dma), c)
            seq = None
        self.snap[tok] = dict(know)
        self.snap[tok][("E", e)] = self.seq[e]
        self.ops[e].append(_Op(emit, waits, tok, seq))
        for r in reads:
            st = self.res.setdefault(r, _Res())
            if st.reads.get(tok[0], 0) < tok[1]:
                st.reads[tok[0]] = tok[1]
        for w in writes:
            st = self.res.setdefault(w, _Res())
            st.writes = {tok[0]: tok[1]}
            st.reads = {}
        return tok

    def arena_view(self, name, off, size):
        st = self.res.setdefault(name, _Res())
        for (oname, ooff, osize) in self.arena:
            if oname == name:
                continue
            if off < ooff + osize and ooff < off + osize:
                ost = self.res.get(oname)
                if ost:
                    for k, v in list(ost.writes.items()) + list(ost.reads.items()):
                        if st.writes.get(k, 0) < v:
                            st.writes[k] = v
        self.arena = [a for a in self.arena if a[0] != name] + [(name, off, size)]

    def finalize(self, nc):
        eng_sem = {e: nc.alloc_semaphore(f"sem_{e}") for e in ENGINES}
        dma_sem = {s: nc.alloc_semaphore(f"dma_{s}") for s in self.dma_count}
        waited = {e: set() for e in ENGINES}
        for e in ENGINES:
            for o in self.ops[e]:
                for k, v in o.waits:
                    if k[0] == "E":
                        waited[k[1]].add(v)
        wlist = {e: sorted(waited[e]) for e in ENGINES}

        def emit_stream(eng, ename):
            for o in self.ops[ename]:
                for k, v in o.waits:
                    if k[0] == "D":
                        eng.wait_ge(dma_sem[k[1]], v)
                    else:
                        nval = bisect.bisect_right(wlist[k[1]], v)
                        eng.wait_ge(eng_sem[k[1]], nval)
                inst = o.emit(eng)
                if o.seq is not None and o.seq in waited[ename]:
                    inst.then_inc(eng_sem[ename], 1)
                elif o.seq is None:
                    inst.then_inc(dma_sem[o.tok[0][1]], 16)
            if ename == "sync":
                for s, c in self.dma_count.items():
                    eng.wait_ge(dma_sem[s], c)

        block = nc.Block()
        block.__enter__()
        try:
            for ename in ENGINES:
                getattr(block, ename)(
                    (lambda eng, _n=ename: emit_stream(eng, _n)))
        finally:
            block.__exit__(None, None, None)



def build_nc(phases=("ffn1", "attn", "ffn2"), dbg_ln_chunks=NCH, dbg_skip=()):
    nc = bass.Bass("TRN2", target_bir_lowering=False, debug=False,
                   detect_race_conditions=False)
    P = Prog()

    dr = {}
    def dt_in(name, shape, dtype=F32):
        dr[name] = nc.dram_tensor(name, shape, dtype, kind="ExternalInput").ap()
    dt_in("x0", [NCH, 128, D])
    dt_in("wkqv", [H, 8, 128, 2112])
    dt_in("wp", [8, 128, D])
    dt_in("bkq", [128, 256])
    dt_in("bv", [1, H * DH])
    dt_in("w1l", [4, 128, 8, 128])
    dt_in("w2l", [4, 128, D])
    dt_in("w1h", [32, 128, 8, 128])
    dt_in("w2h", [32, 128, D])
    dt_in("b1h", [128, 32])
    dt_in("b1l", [128, 4])
    dt_in("b2l", [1, D])
    dt_in("b2h", [1, D])
    dt_in("bp", [1, D])
    dt_in("gl", [128, 8])
    dt_in("gh", [128, 8])
    dt_in("rrep", [128, D])
    dt_in("cmask", [128, 896], BF16)
    dt_in("dneg", [128, 128], BF16)
    dt_in("i128bf", [128, 128], BF16)
    dt_in("i128f", [128, 128])
    dt_in("ones1bf", [1, 128], BF16)
    dt_in("ones1f", [1, 128])
    dt_in("ones128f", [128, 128])
    dt_in("l128f", [128, 128])
    dt_in("i16f", [16, 16])
    dt_in("l16f", [16, 16])
    out_dr = nc.dram_tensor("out", [NCH, 128, D], F32, kind="ExternalOutput").ap()
    if DBG:
        dbg_dr = nc.dram_tensor("dbg", [64, 128, 2048], F32, kind="ExternalOutput").ap()
    dbg_i = [0]
    def dump(buf_ap, res_list, cols):
        i = dbg_i[0]; dbg_i[0] += 1
        P.op("vector", lambda eng, buf_ap=buf_ap, cols=cols: eng.tensor_copy(
            T["dbgstage"][:, 0:cols], buf_ap), reads=list(res_list) + ["dbgstage"],
            writes=["dbgstage"])
        dma("sync", "dbgst", dbg_dr[i], T["dbgstage"][:],
            ["dbgstage"], [f"dbg{i}"])
    x1_dr = nc.dram_tensor("x1buf", [NCH, 128, D], F32, kind="Internal").ap()
    x2_dr = nc.dram_tensor("x2buf", [NCH, 128, D], F32, kind="Internal").ap()
    w1hbf_dr = nc.dram_tensor("w1hbf", [32, 128, D], BF16, kind="Internal").ap()
    w2hbf_dr = nc.dram_tensor("w2hbf", [32, 128, D], BF16, kind="Internal").ap()

    sb = {}
    def alloc(name, shape, dtype=F32):
        sb[name] = nc.alloc_sbuf_tensor(name, shape, dtype)
        return sb[name]

    alloc("act", [128, 8, N], BF16)
    alloc("xs0", [128, D]); alloc("xs1", [128, D])
    alloc("xr0", [128, D]); alloc("xr1", [128, D])
    alloc("ln0", [128, D])
    alloc("ou0", [128, D]); alloc("ou1", [128, D])
    for nm in ("sumx", "ssq", "mu", "var", "rstd", "tsc", "t16a", "t16b",
               "s16", "sb16", "u16", "z16", "u01", "cg16", "ci16", "mask16"):
        alloc(nm, [128, 16])
    for nm in ("junk1", "part1", "a1", "c1", "l1", "m0", "m1", "negm", "S1",
               "rS", "cp0", "cp1", "d0", "d1", "sp0", "sp1", "sp2", "sp3"):
        alloc(nm, [128, 1])
    alloc("db0", [128, 128], BF16); alloc("db1", [128, 128], BF16)
    alloc("junk16", [128, 16])
    alloc("t16c", [16, 1]); alloc("offs", [1, 16])
    alloc("c_i128bf", [128, 128], BF16)
    alloc("c_i128f", [128, 128])
    alloc("c_ones1bf", [1, 128], BF16)
    alloc("c_ones1f", [1, 128])
    alloc("c_ones128f", [128, 128])
    alloc("c_l128f", [128, 128])
    alloc("c_i16f", [16, 16])
    alloc("c_l16f", [16, 16])
    alloc("c_cm", [128, 896], BF16)
    alloc("c_dneg", [128, 128], BF16)
    alloc("c_bkq", [128, 256])
    alloc("c_b1h", [128, 32])
    alloc("c_b1l", [128, 4])
    alloc("c_gl", [128, 8])
    alloc("c_gh", [128, 8])
    alloc("c_b2l", [1, D], BF16)
    alloc("c_b2h", [1, D], BF16)
    alloc("c_bp", [1, D], BF16)
    alloc("c_bv", [1, H * DH], BF16)

    ARENA = 117760
    F32ARENA = 16896
    arena_t = nc.alloc_sbuf_tensor("arena", [128, ARENA // 2], BF16)
    arena_f = nc.alloc_sbuf_tensor("arenaf", [128, F32ARENA // 4], F32)
    def aview(name, off, shape, dtype):
        nel = int(np.prod(shape[1:]))
        isz = 2 if dtype == BF16 else 4
        nbytes = nel * isz
        if dtype == BF16:
            ap = arena_t.ap()[:, off // 2:(off + nbytes) // 2]
        else:
            ap = arena_f.ap()[:, off // 4:(off + nbytes) // 4]
        if len(shape) == 3:
            ap = ap.rearrange("p (a b) -> p a b", a=shape[1])
        elif len(shape) == 4:
            ap = ap.rearrange("p (a b c) -> p a b c", a=shape[1], b=shape[2])
        if shape[0] != 128:
            ap = ap[0:shape[0]]
        sb[name] = ap
        return off + nbytes
    aview("h1", 0, [128, 32, 512], BF16)
    aview("w1s", 32768, [128, 2, D], BF16)
    aview("w2s", 36864, [128, 2, D], BF16)
    aview("h1l", 40960, [128, 4, 512], BF16)
    aview("lw1", 45056, [128, 4, 8, 128], BF16)
    aview("lw2", 53248, [128, 4, D], BF16)
    aview("lout", 61440, [128, 4, D], BF16)
    aview("junk2k", 69632, [128, N], BF16)
    aview("c_gfhb", 73728, [128, D], BF16)
    aview("K", 0, [128, 8, N], BF16)
    aview("sa", 32768, [128, 8, N], BF16)
    aview("wkv", 65536, [128, 8, 2112], BF16)
    aview("wpb", 65536, [128, 8, D], BF16)
    aview("Q", 99328, [128, 8, 512], BF16)
    aview("V", 107520, [128, 16, DH], BF16)
    aview("Pb", 109568, [128, N], BF16)
    aview("PT", 113664, [128, N], BF16)
    aview("ys", 113664, [128, D], BF16)
    aview("zR", 0, [128, N], F32)
    aview("tmpc", 0, [128, D], F32)
    aview("c_rrep", 8192, [128, D], F32)
    aview("zflat", 8192, [1, N], F32)
    aview("zt", 16384, [16, 128], F32)
    if DBG:
        aview("dbgstage", 0, [128, 2048], F32)
    FFN_VIEWS = {"h1": (0, 32768), "w1s": (32768, 4096), "w2s": (36864, 4096),
                 "h1l": (40960, 4096), "lw1": (45056, 8192), "lw2": (53248, 8192),
                 "lout": (61440, 8192), "junk2k": (69632, 4096),
                 "ys": (113664, 2048)}
    F32_VIEWS = {"zR": (1 << 24, 8192), "tmpc": (1 << 24, 4096),
                 "c_rrep": ((1 << 24) + 8192, 4096), "zt": ((1 << 24) + 16384, 512)}
    ZFLAT_VIEW = ((1 << 24) + 8192, 8192)
    ATTN_VIEWS = {"K": (0, 32768), "sa": (32768, 32768), "wkv": (65536, 33792),
                  "Q": (99328, 8192), "V": (107520, 2048), "Pb": (109568, 4096),
                  "PT": (113664, 4096)}
    WP_VIEW = {"wpb": (65536, 16384)}

    pband = []
    for i in range(8):
        pband.append(nc.alloc_psum_tensor(f"ps{i}", [128, 512], F32))

    class Pool:
        def __init__(self, idxs):
            self.idxs = idxs
            self.i = 0
        def alloc(self):
            j = self.idxs[self.i % len(self.idxs)]
            self.i += 1
            return pband[j], f"ps{j}"
    pool = Pool([0, 1, 2, 3, 4])
    ptpool = Pool([5, 6])

    T = sb
    def dma(e, stream, dst, src, reads, writes):
        P.op(e, lambda eng, d=dst, s=src: eng.dma_start(out=d, in_=s),
             reads=reads, writes=writes, dma=stream)

    for nm, srcn in (("c_ones128f", "ones128f"), ("c_l128f", "l128f"),
                    ("c_i16f", "i16f"), ("c_l16f", "l16f"), ("c_bkq", "bkq"),
                    ("c_b1h", "b1h"), ("c_b1l", "b1l"), ("c_gl", "gl"),
                    ("c_i128f", "i128f"), ("c_ones1f", "ones1f")):
        dma("sync", "cst", T[nm][:], dr[srcn], [], [nm])
    dma("sync", "cst", T["c_cm"][:], dr["cmask"], [], ["c_cm"])
    dma("sync", "cst", T["c_i128bf"][:], dr["i128bf"], [], ["c_i128bf"])
    dma("sync", "cst", T["c_ones1bf"][:], dr["ones1bf"], [], ["c_ones1bf"])
    for nm, src in (("c_b2l", "b2l"), ("c_b2h", "b2h"), ("c_bp", "bp"),
                    ("c_bv", "bv")):
        dma("gpsimd", "lw", T[nm][:], dr[src], [], [nm])
    P.arena_view("c_gfhb", 73728, 2048)
    dma("sync", "cst", T["c_gh"][:], dr["gh"], [], ["c_gh"])
    P.op("vector", lambda eng: eng.memset(T["c_gfhb"][:], 1.0),
         reads=[], writes=["c_gfhb"])
    for dc in range(8):
        P.op("vector", lambda eng, dc=dc: eng.tensor_scalar(
            T["c_gfhb"][:, dc * 128:(dc + 1) * 128],
            T["c_gfhb"][:, dc * 128:(dc + 1) * 128],
            T["c_gh"][:, dc:dc + 1], None, ALU.mult),
            reads=["c_gfhb", "c_gh"], writes=["c_gfhb"])

    def load_light_weights():
        for v in ("lw1", "lw2"):
            P.arena_view(v, *FFN_VIEWS[v])
        for dlc in range(4):
            dma("gpsimd", "lw", T["lw1"][:, dlc, :, :], dr["w1l"][dlc],
                ["lw1"], ["lw1"])
            dma("gpsimd", "lw", T["lw2"][:, dlc, :], dr["w2l"][dlc],
                ["lw2"], ["lw2"])
        for dc in range(8):
            P.op("vector",
                 lambda eng, dc=dc: eng.tensor_scalar(
                     T["lw1"][:, :, dc, :], T["lw1"][:, :, dc, :],
                     T["c_gl"][:, dc:dc + 1], None, ALU.mult),
                 reads=["lw1", "c_gl"], writes=["lw1"])

    def ln_phase(src, srcpre, with_ffn):
        for c in range(dbg_ln_chunks):
            xs = T["xs0"] if c % 2 == 0 else T["xs1"]
            xsr = f"xs{c % 2}"
            lnb = T["ln0"]
            lnr = "ln0"
            dma("sync", f"xld{c % 2}", xs[:], src[c],
                [f"{srcpre}{c}"] if srcpre else [], [xsr])
            cc = slice(c, c + 1)
            P.op("scalar", lambda eng, xs=xs, cc=cc: eng.activation(
                T["tmpc"][:], xs[:], AF.Square, accum_out=T["ssq"][:, cc]),
                reads=[xsr], writes=["tmpc", f"ssq{c}"])
            P.op("scalar", lambda eng, xs=xs, lnb=lnb, cc=cc: eng.activation(
                lnb[:], xs[:], AF.Identity, accum_out=T["sumx"][:, cc]),
                reads=[xsr], writes=[lnr, f"sumx{c}"])
            P.op("vector", lambda eng, cc=cc: eng.tensor_scalar(
                T["mu"][:, cc], T["sumx"][:, cc], 1.0 / D, None, ALU.mult),
                reads=[f"sumx{c}"], writes=[f"mu{c}"])
            P.op("vector", lambda eng, cc=cc: eng.tensor_tensor(
                T["t16a"][:, cc], T["mu"][:, cc], T["mu"][:, cc], ALU.mult),
                reads=[f"mu{c}"], writes=[f"t16a{c}"])
            P.op("vector", lambda eng, cc=cc: eng.scalar_tensor_tensor(
                T["var"][:, cc], T["ssq"][:, cc], 1.0 / D, T["t16a"][:, cc],
                ALU.mult, ALU.subtract), reads=[f"ssq{c}", f"t16a{c}"], writes=[f"var{c}"])
            P.op("scalar", lambda eng, cc=cc: eng.activation(
                T["t16b"][:, cc], T["var"][:, cc], AF.Ln, bias=np.float32(EPS)),
                reads=[f"var{c}"], writes=[f"t16b{c}"])
            P.op("scalar", lambda eng, cc=cc: eng.activation(
                T["rstd"][:, cc], T["t16b"][:, cc], AF.Exp, scale=-0.5,
                accum_out=T["junk1"][:]), reads=[f"t16b{c}"],
                writes=[f"rstd{c}"])
            P.op("vector", lambda eng, xs=xs, lnb=lnb, cc=cc: eng.tensor_scalar(
                lnb[:], xs[:], T["mu"][:, cc], T["rstd"][:, cc],
                ALU.subtract, ALU.mult), reads=[xsr, f"mu{c}", f"rstd{c}"], writes=[lnr])
            if with_ffn:
                P.op("vector", lambda eng, lnb=lnb: eng.tensor_tensor(
                    T["tmpc"][:], lnb[:], T["c_rrep"][:], ALU.mult),
                    reads=[lnr, "c_rrep"], writes=["tmpc"])
                P.op("vector", lambda eng, cc=cc: eng.reduce_sum(
                    T["s16"][:, cc], T["tmpc"][:], AX),
                    reads=["tmpc"], writes=[f"s16_{c}"])
                P.op("vector", lambda eng, cc=cc: eng.tensor_tensor(
                    T["t16a"][:, cc], T["rstd"][:, cc], T["rstd"][:, cc],
                    ALU.mult), reads=[f"rstd{c}"], writes=[f"t16a{c}"])
                P.op("vector", lambda eng, cc=cc: eng.scalar_tensor_tensor(
                    T["t16b"][:, cc], T["var"][:, cc], float(D), T["t16a"][:, cc],
                    ALU.mult, ALU.mult), reads=[f"var{c}", f"t16a{c}"], writes=[f"t16b{c}"])
                P.op("vector", lambda eng, cc=cc: eng.tensor_scalar(
                    T["t16b"][:, cc], T["t16b"][:, cc], 1e-24, None, ALU.max),
                    reads=[f"t16b{c}"], writes=[f"t16b{c}"])
                P.op("scalar", lambda eng, cc=cc: eng.activation(
                    T["t16a"][:, cc], T["t16b"][:, cc], AF.Ln, scale=1.0 / 1024.0,
                    accum_out=T["junk1"][:]),
                    reads=[f"t16b{c}"], writes=[f"t16a{c}"])
                P.op("scalar", lambda eng, cc=cc: eng.activation(
                    T["tsc"][:, cc], T["t16a"][:, cc], AF.Exp, scale=-0.5,
                    accum_out=T["junk1"][:]),
                    reads=[f"t16a{c}"], writes=[f"tsc{c}"])
                P.op("vector", lambda eng, lnb=lnb, cc=cc: eng.tensor_scalar(
                    T["ys"][:], lnb[:], T["tsc"][:, cc], None, ALU.mult),
                    reads=[lnr, f"tsc{c}"], writes=["ys"])
            else:
                P.op("vector", lambda eng, lnb=lnb: eng.tensor_copy(
                    T["ys"][:], lnb[:]), reads=[lnr, "ys"], writes=["ys"])
            for g in range(2):
                pt, ptr = pool.alloc()
                for k in range(4):
                    dc = g * 4 + k
                    P.op("tensor", lambda eng, pt=pt, dc=dc, k=k: eng.matmul(
                        pt[:, k * 128:(k + 1) * 128],
                        T["ys"][:, dc * 128:(dc + 1) * 128], T["c_i128bf"][:],
                        start=True, stop=True),
                        reads=["ys", "c_i128bf", ptr], writes=[ptr])
                P.op("scalar", lambda eng, pt=pt, g=g, c=c: eng.copy(
                    T["act"][:, g * 4:(g + 1) * 4, c * 128:(c + 1) * 128],
                    pt[:].rearrange("p (a b) -> p a b", a=4)),
                    reads=[ptr], writes=[f"act{g}_{c // 4}"])

    def coor_mask_units():
        P.op("vector", lambda eng: eng.memset(T["sb16"][:], 0.0),
             reads=[f"s16_{c}" for c in range(NCH)], writes=["sb16"])
        P.op("vector", lambda eng: eng.memset(T["c1"][:], LOGK),
             reads=[f"s16_{c}" for c in range(NCH)], writes=["c1", "s16x"])
        yield
        for it in range(50):
            P.op("scalar", lambda eng: eng.activation(
                T["junk16"][:], T["sb16"][:], AF.Exp,
                accum_out=T["part1"][:]),
                reads=["sb16"], writes=["part1"])
            pt, ptr = pool.alloc()
            P.op("tensor", lambda eng, pt=pt: eng.matmul(
                pt[:, 0:1], T["c_ones128f"][:], T["part1"][:],
                start=True, stop=True),
                reads=["part1", "c_ones128f", ptr], writes=[ptr])
            P.op("scalar", lambda eng, pt=pt: eng.activation(
                T["l1"][:], pt[:, 0:1], AF.Ln), reads=[ptr], writes=["l1"])
            P.op("vector", lambda eng: eng.tensor_scalar(
                T["a1"][:], T["l1"][:], -1.0, T["c1"][:, 0:1], ALU.mult, ALU.add),
                reads=["l1", "c1"], writes=["a1"])
            P.op("vector", lambda eng: eng.tensor_scalar(
                T["c1"][:], T["a1"][:], LOGK, None, ALU.add),
                reads=["a1"], writes=["c1"])
            P.op("vector", lambda eng: eng.tensor_scalar(
                T["sb16"][:], T["s16"][:], T["a1"][:, 0:1], 0.0,
                ALU.add, ALU.min), reads=["s16x", "a1"], writes=["sb16"])
            yield
        P.op("vector", lambda eng: eng.tensor_scalar(
            T["u16"][:], T["s16"][:], T["a1"][:, 0:1], None, ALU.add),
            reads=["s16x", "a1"], writes=["u16"])
        P.op("vector", lambda eng: eng.tensor_scalar(
            T["t16a"][:], T["u16"][:], 0.0, None, ALU.max),
            reads=["u16"], writes=["t16a"])
        P.op("vector", lambda eng: eng.tensor_tensor(
            T["z16"][:], T["u16"][:], T["t16a"][:], ALU.subtract),
            reads=["u16", "t16a"], writes=["z16"])
        P.op("vector", lambda eng: eng.tensor_scalar(
            T["u01"][:], T["z16"][:], 0.0, None, ALU.is_equal),
            reads=["z16"], writes=["u01"])
        yield
        pt, ptr = pool.alloc()
        P.op("tensor", lambda eng, pt=pt: eng.matmul(
            pt[0:16, 0:128], T["z16"][:], T["c_i128f"][:],
            start=True, stop=True), reads=["z16", "c_i128f", ptr], writes=[ptr])
        P.op("vector", lambda eng, pt=pt: eng.tensor_copy(
            T["zt"][:], pt[0:16, 0:128]), reads=[ptr], writes=["zt"])
        for c in range(NCH):
            dma("sync", "zf", T["zflat"][:, c * 128:(c + 1) * 128],
                T["zt"][c:c + 1, :], ["zt"], [f"zflat{c}"])
        yield
        for j in range(4):
            pt2, ptr2 = pool.alloc()
            P.op("tensor", lambda eng, pt2=pt2, j=j: eng.matmul(
                pt2[:], T["c_ones1f"][:], T["zflat"][:, j * 512:(j + 1) * 512],
                start=True, stop=True),
                reads=[f"zflat{4 * j}", f"zflat{4 * j + 1}", f"zflat{4 * j + 2}",
                       f"zflat{4 * j + 3}", "c_ones1f", ptr2], writes=[ptr2])
            P.op("vector", lambda eng, pt2=pt2, j=j: eng.tensor_copy(
                T["zR"][:, j * 512:(j + 1) * 512], pt2[:]),
                reads=[ptr2, "zR"], writes=["zR"])
            yield
        for c in range(NCH):
            P.op("vector", lambda eng, c=c: eng.tensor_scalar(
                T["junk2k"][:], T["zR"][:], T["z16"][:, c:c + 1], None,
                ALU.is_gt, ALU.add, accum_out=T["cg16"][:, c:c + 1]),
                reads=["zR", "z16"], writes=["cg16"])
            if c % 4 == 3:
                yield
        pt3, ptr3 = pool.alloc()
        P.op("tensor", lambda eng, pt3=pt3: eng.matmul(
            pt3[:, 0:16], T["c_l128f"][:], T["u01"][:], start=True, stop=True),
            reads=["u01", "c_l128f", ptr3], writes=[ptr3])
        P.op("vector", lambda eng, pt3=pt3: eng.tensor_copy(
            T["t16a"][:], pt3[:, 0:16]), reads=[ptr3], writes=["t16a"])
        yield
        pt4, ptr4 = pool.alloc()
        P.op("tensor", lambda eng, pt4=pt4: eng.matmul(
            pt4[0:16, 0:1], T["u01"][:], T["c_ones128f"][:, 0:1],
            start=True, stop=True), reads=["u01", "c_ones128f", ptr4], writes=[ptr4])
        P.op("vector", lambda eng, pt4=pt4: eng.tensor_copy(
            T["t16c"][:], pt4[0:16, 0:1]), reads=[ptr4], writes=["t16c"])
        pt5, ptr5 = pool.alloc()
        P.op("tensor", lambda eng, pt5=pt5: eng.matmul(
            pt5[0:16, 0:1], T["c_l16f"][:], T["t16c"][:], start=True, stop=True),
            reads=["t16c", "c_l16f", ptr5], writes=[ptr5])
        P.op("vector", lambda eng, pt5=pt5: eng.tensor_copy(
            T["t16c"][:], pt5[0:16, 0:1]), reads=[ptr5, "t16c"], writes=["t16c"])
        yield
        pt6, ptr6 = pool.alloc()
        P.op("tensor", lambda eng, pt6=pt6: eng.matmul(
            pt6[0:1, 0:16], T["t16c"][:], T["c_i16f"][:], start=True, stop=True),
            reads=["t16c", "c_i16f", ptr6], writes=[ptr6])
        P.op("vector", lambda eng, pt6=pt6: eng.tensor_copy(
            T["offs"][:], pt6[0:1, 0:16]), reads=[ptr6], writes=["offs"])
        pt7, ptr7 = pool.alloc()
        P.op("tensor", lambda eng, pt7=pt7: eng.matmul(
            pt7[:, 0:16], T["c_ones1f"][:], T["offs"][:], start=True, stop=True),
            reads=["offs", "c_ones1f", ptr7], writes=[ptr7])
        P.op("vector", lambda eng, pt7=pt7: eng.tensor_tensor(
            T["ci16"][:], T["t16a"][:], pt7[:, 0:16], ALU.add),
            reads=["t16a", ptr7], writes=["ci16"])
        P.op("vector", lambda eng: eng.tensor_tensor(
            T["t16b"][:], T["u01"][:], T["ci16"][:], ALU.mult),
            reads=["u01", "ci16"], writes=["t16b"])
        P.op("vector", lambda eng: eng.tensor_tensor(
            T["t16a"][:], T["t16b"][:], T["cg16"][:], ALU.add),
            reads=["t16b", "cg16"], writes=["t16a"])
        P.op("vector", lambda eng: eng.tensor_scalar(
            T["mask16"][:], T["t16a"][:], 1024.0, None, ALU.is_lt),
            reads=["t16a"], writes=["mask16"])

    def ffn_compute(srcdram, srcpre, dstdram, dstpre, coor=None, convert=False):
        def drain(n):
            if coor is not None:
                for _ in range(n):
                    if next(coor, "END") == "END":
                        break
        for q in range(4):
            for dhc in range(32):
                slot = dhc % 2
                if convert and q == 0:
                    xs = T["xs0"] if slot == 0 else T["xs1"]
                    if dhc < 2:
                        dma("sync", f"wld{slot}", xs[:],
                            dr["w1h"][dhc].rearrange("p a b -> p (a b)"),
                            [], [f"xs{slot}"])
                    P.op("vector", lambda eng, xs=xs, s=slot: eng.tensor_copy(
                        T["w1s"][:, s, :], xs[:]),
                        reads=[f"xs{slot}"], writes=[f"w1s{slot}"])
                    if dhc + 2 < 32:
                        dma("sync", f"wld{slot}", xs[:],
                            dr["w1h"][dhc + 2].rearrange("p a b -> p (a b)"),
                            [], [f"xs{slot}"])
                    P.op("vector", lambda eng, s=slot: eng.tensor_tensor(
                        T["w1s"][:, s, :], T["w1s"][:, s, :], T["c_gfhb"][:],
                        ALU.mult),
                        reads=[f"w1s{slot}", "c_gfhb"], writes=[f"w1s{slot}"])
                else:
                    dma("sync", f"w1f{slot}", T["w1s"][:, slot, :], w1hbf_dr[dhc],
                        [f"w1hbf{dhc}"], [f"w1s{slot}"])
                pt, ptr = pool.alloc()
                for dc in range(8):
                    P.op("tensor", lambda eng, pt=pt, s=slot, dc=dc, q=q: eng.matmul(
                        pt[:], T["w1s"][:, s, dc * 128:(dc + 1) * 128],
                        T["act"][:, dc, q * 512:(q + 1) * 512],
                        start=(dc == 0), stop=(dc == 7)),
                        reads=[f"w1s{slot}", f"act{dc // 4}_{q}", ptr], writes=[ptr])
                P.op("scalar", lambda eng, pt=pt, dhc=dhc: eng.activation(
                    T["h1"][:, dhc, :], pt[:], AF.Gelu,
                    bias=T["c_b1h"][:, dhc:dhc + 1]),
                    reads=[ptr, "c_b1h"], writes=[f"h1_{dhc}"])
                if convert and q == 0:
                    dma("gpsimd", f"wst{slot}", w1hbf_dr[dhc], T["w1s"][:, slot, :],
                        [f"w1s{slot}"], [f"w1hbf{dhc}"])
                if q == 0:
                    drain(2)
            for dlc in range(4):
                pt, ptr = pool.alloc()
                for dc in range(8):
                    P.op("tensor", lambda eng, pt=pt, dlc=dlc, dc=dc, q=q: eng.matmul(
                        pt[:], T["lw1"][:, dlc, dc, :],
                        T["act"][:, dc, q * 512:(q + 1) * 512],
                        start=(dc == 0), stop=(dc == 7)),
                        reads=["lw1", f"act{dc // 4}_{q}", ptr], writes=[ptr])
                P.op("scalar", lambda eng, pt=pt, dlc=dlc: eng.activation(
                    T["h1l"][:, dlc, :], pt[:], AF.Gelu,
                    bias=T["c_b1l"][:, dlc:dlc + 1]),
                    reads=[ptr, "c_b1l"], writes=[f"h1l_{dlc}"])
                if q == 0:
                    drain(2)
            for ncq in range(4):
                for hd in range(2):
                    pt, ptr = pool.alloc()
                    for dlc in range(4):
                        P.op("tensor", lambda eng, pt=pt, dlc=dlc, ncq=ncq, hd=hd: eng.matmul(
                            pt[:], T["h1l"][:, dlc, ncq * 128:(ncq + 1) * 128],
                            T["lw2"][:, dlc, hd * 512:(hd + 1) * 512],
                            start=(dlc == 0), stop=False),
                            reads=[f"h1l_{dlc}", "lw2", ptr], writes=[ptr])
                    P.op("tensor", lambda eng, pt=pt, hd=hd: eng.matmul(
                        pt[:], T["c_ones1bf"][:], T["c_b2l"][:, hd * 512:(hd + 1) * 512],
                        start=False, stop=True),
                        reads=["c_ones1bf", "c_b2l", ptr], writes=[ptr])
                    P.op("vector", lambda eng, pt=pt, ncq=ncq, hd=hd: eng.tensor_copy(
                        T["lout"][:, ncq, hd * 512:(hd + 1) * 512], pt[:]),
                        reads=[ptr], writes=[f"lout{ncq}_{hd}"])
                    if q == 0:
                        drain(2)
            if q == 0:
                drain(10 ** 6)
            for ncq in (0, 1):
                c = q * 4 + ncq
                xr = T["xr0"] if c % 2 == 0 else T["xr1"]
                dma("sync", f"xrs{c % 2}", xr[:], srcdram[c],
                    [f"{srcpre}{c}"] if srcpre else [], [f"xr{c % 2}"])
            for dhc in range(32):
                slot = dhc % 2
                if convert and q == 0:
                    xs = T["xs0"] if slot == 0 else T["xs1"]
                    if dhc < 2:
                        dma("sync", f"wld{slot}", xs[:], dr["w2h"][dhc],
                            [], [f"xs{slot}"])
                    P.op("vector", lambda eng, xs=xs, s=slot: eng.tensor_copy(
                        T["w2s"][:, s, :], xs[:]),
                        reads=[f"xs{slot}"], writes=[f"w2s{slot}"])
                    if dhc + 2 < 32:
                        dma("sync", f"wld{slot}", xs[:], dr["w2h"][dhc + 2],
                            [], [f"xs{slot}"])
                else:
                    dma("sync", f"w2f{slot}", T["w2s"][:, slot, :], w2hbf_dr[dhc],
                        [f"w2hbf{dhc}"], [f"w2s{slot}"])
                for ncq in range(4):
                    for hd in range(2):
                        b = ncq * 2 + hd
                        P.op("tensor", lambda eng, b=b, dhc=dhc, s=slot, ncq=ncq, hd=hd: eng.matmul(
                            pband[b][:], T["h1"][:, dhc, ncq * 128:(ncq + 1) * 128],
                            T["w2s"][:, s, hd * 512:(hd + 1) * 512],
                            start=(dhc == 0), stop=False),
                            reads=[f"h1_{dhc}", f"w2s{slot}", f"ps{b}"], writes=[f"ps{b}"])
                if convert and q == 0:
                    dma("gpsimd", f"wst{slot}", w2hbf_dr[dhc], T["w2s"][:, slot, :],
                        [f"w2s{slot}"], [f"w2hbf{dhc}"])
            for ncq in range(4):
                for hd in range(2):
                    b = ncq * 2 + hd
                    P.op("tensor", lambda eng, b=b, hd=hd: eng.matmul(
                        pband[b][:], T["c_ones1bf"][:],
                        T["c_b2h"][:, hd * 512:(hd + 1) * 512],
                        start=False, stop=True),
                        reads=["c_ones1bf", "c_b2h", f"ps{b}"], writes=[f"ps{b}"])
            for ncq in range(4):
                c = q * 4 + ncq
                xr = T["xr0"] if c % 2 == 0 else T["xr1"]
                xrr = f"xr{c % 2}"
                ou = T["ou0"] if c % 2 == 0 else T["ou1"]
                our = f"ou{c % 2}"
                for hd in range(2):
                    b = ncq * 2 + hd
                    sl = slice(hd * 512, (hd + 1) * 512)
                    P.op("vector", lambda eng, b=b, ncq=ncq, sl=sl, c=c: eng.scalar_tensor_tensor(
                        T["tmpc"][:, sl], pband[b][:], T["mask16"][:, c:c + 1],
                        T["lout"][:, ncq, sl], ALU.mult, ALU.add),
                        reads=[f"ps{b}", "mask16", f"lout{ncq}_{hd}"], writes=["tmpc"])
                    P.op("vector", lambda eng, xr=xr, ou=ou, sl=sl: eng.scalar_tensor_tensor(
                        ou[:, sl], T["tmpc"][:, sl], 0.5, xr[:, sl],
                        ALU.mult, ALU.add),
                        reads=["tmpc", xrr, our], writes=[our])
                dma("sync", f"st{c % 2}", dstdram[c], ou[:], [our], [f"{dstpre}{c}"])
                if ncq < 2 and q * 4 + ncq + 2 < 16:
                    c2 = q * 4 + ncq + 2
                    xr2 = T["xr0"] if c2 % 2 == 0 else T["xr1"]
                    dma("sync", f"xrs{c2 % 2}", xr2[:], srcdram[c2],
                        [f"{srcpre}{c2}"] if srcpre else [], [f"xr{c2 % 2}"])

    def ffn_views():
        for v, (o, s) in FFN_VIEWS.items():
            if v == "lout":
                for ncq in range(4):
                    for hd in range(2):
                        P.arena_view(f"lout{ncq}_{hd}", o, s)
            elif v == "h1":
                for dhc in range(32):
                    P.arena_view(f"h1_{dhc}", o, s)
            elif v == "h1l":
                for dlc in range(4):
                    P.arena_view(f"h1l_{dlc}", o, s)
            else:
                P.arena_view(v, o, s)
        for v, (o, s) in F32_VIEWS.items():
            P.arena_view(v, o, s)
        for c in range(NCH):
            P.arena_view(f"zflat{c}", ZFLAT_VIEW[0], ZFLAT_VIEW[1])
        dma("sync", "cst", T["c_rrep"][:], dr["rrep"], [], ["c_rrep"])

    def ffn_phase(srcdram, srcpre, dstdram, dstpre, convert=False):
        ffn_views()
        load_light_weights()
        ln_phase(srcdram, srcpre, True)
        if DBG and DBG_FFN:
            dump(T["ssq"][:], ["ssq"], 16)
            dump(T["var"][:], ["var"], 16)
            dump(T["rstd"][:], ["rstd"], 16)
            dump(T["tsc"][:], ["tsc"], 16)
            dump(T["ln0"][:], ["ln0"], 1024)
            dump(T["ys"][:], ["ys"], 1024)
            for dc in range(8):
                dump(T["act"][:, dc, :], [f"act{dc // 4}_{r}" for r in range(4)], 2048)
            dump(T["s16"][:], ["s16"], 16)
        coor = coor_mask_units()
        if DBG and DBG_FFN:
            for _ in coor:
                pass
            coor = None
            dump(T["a1"][:], ["a1"], 1)
            dump(T["z16"][:], ["z16"], 16)
            dump(T["u01"][:], ["u01"], 16)
            dump(T["cg16"][:], ["cg16"], 16)
            dump(T["ci16"][:], ["ci16"], 16)
            dump(T["mask16"][:], ["mask16"], 16)
            dump(T["zR"][:], ["zR"], 2048)
        ffn_compute(srcdram, srcpre, dstdram, dstpre, coor, convert)
        if DBG and DBG_FFN:
            dump(T["h1"][:, 0, :], ["h1_0"], 512)
            dump(T["h1l"][:, 0, :], ["h1l_0"], 512)
            dump(T["lout"][:, 0, :], ["lout0_0", "lout0_1"], 1024)

    def attn_phase():
        for v, (o, s) in ATTN_VIEWS.items():
            if v == "K":
                for nb in range(4):
                    P.arena_view(f"K{nb}", o, s)
            elif v == "wkv":
                for dc in range(8):
                    P.arena_view(f"wkv{dc}", o, s)
            else:
                P.arena_view(v, o, s)
        ln_phase(x1_dr, "x1:", False)
        for h in range(H):
            for dc in range(8):
                dma("gpsimd", "wkqv", T["wkv"][:, dc, :], dr["wkqv"][h, dc],
                    [], [f"wkv{dc}"])
            for ec in range(8):
                for nb in range(4):
                    pt, ptr = pool.alloc()
                    for dc in range(8):
                        P.op("tensor", lambda eng, pt=pt, dc=dc, ec=ec, nb=nb: eng.matmul(
                            pt[:], T["wkv"][:, dc, ec * 128:(ec + 1) * 128],
                            T["act"][:, dc, nb * 512:(nb + 1) * 512],
                            start=(dc == 0), stop=(dc == 7)),
                            reads=[f"wkv{dc}", f"act{dc // 4}_{nb}", ptr], writes=[ptr])
                    P.op("scalar", lambda eng, pt=pt, ec=ec, nb=nb, h=h: eng.add(
                        T["K"][:, ec, nb * 512:(nb + 1) * 512], pt[:],
                        T["c_bkq"][:, h * 8 + ec:h * 8 + ec + 1]),
                        reads=[ptr, "c_bkq"], writes=[f"K{ec}_{nb}"])
            for g in range(2):
                pt, ptr = pool.alloc()
                for j in range(8):
                    nc_ = g * 8 + j
                    dcc = nc_ // 2
                    for dc in range(8):
                        P.op("tensor", lambda eng, pt=pt, j=j, nc_=nc_, dc=dc: eng.matmul(
                            pt[:, j * 64:(j + 1) * 64],
                            T["act"][:, dc, nc_ * 128:(nc_ + 1) * 128],
                            T["wkv"][:, dc, 2048:2112],
                            start=(dc == 0), stop=False),
                            reads=["act%d_%d" % (dc // 4, nc_ // 4), f"wkv{dc}", ptr], writes=[ptr])
                    P.op("tensor", lambda eng, pt=pt, j=j, h=h: eng.matmul(
                        pt[:, j * 64:(j + 1) * 64], T["c_ones1bf"][:],
                        T["c_bv"][:, h * 64:(h + 1) * 64],
                        start=False, stop=True),
                        reads=["c_ones1bf", "c_bv", ptr], writes=[ptr])
                P.op("scalar", lambda eng, pt=pt, g=g: eng.copy(
                    T["V"][:, g * 8:(g + 1) * 8, :],
                    pt[:].rearrange("p (a b) -> p a b", a=8)),
                    reads=[ptr], writes=[f"V{g}"])
            deferred = [None]
            def flush():
                if deferred[0] is not None:
                    deferred[0]()
                    deferred[0] = None
            for qi in range(16):
                if qi % 4 == 0:
                    qq = qi // 4
                    for ec in range(8):
                        pt, ptr = pool.alloc()
                        for dc in range(8):
                            P.op("tensor", lambda eng, pt=pt, dc=dc, ec=ec, qq=qq: eng.matmul(
                                pt[:], T["wkv"][:, dc, 1024 + ec * 128:1024 + (ec + 1) * 128],
                                T["act"][:, dc, qq * 512:(qq + 1) * 512],
                                start=(dc == 0), stop=(dc == 7)),
                                reads=[f"wkv{dc}", f"act{dc // 4}_{qq}", ptr], writes=[ptr])
                        P.op("vector", lambda eng, pt=pt, ec=ec, h=h: eng.tensor_scalar(
                            T["Q"][:, ec, :], pt[:],
                            T["c_bkq"][:, 128 + h * 8 + ec:128 + h * 8 + ec + 1],
                            0.03125, ALU.add, ALU.mult),
                            reads=[ptr, "c_bkq"], writes=[f"Q{ec}"])
                kbs = qi // 4 + 1
                passes = 1 if kbs <= 2 else 2
                ltiles = []
                for kb in range(kbs):
                    pt, ptr = pool.alloc()
                    ltiles.append((pt, ptr))
                    last = kb == kbs - 1
                    for ec in range(8):
                        P.op("tensor", lambda eng, pt=pt, ec=ec, kb=kb, qi=qi, last=last: eng.matmul(
                            pt[:], T["Q"][:, ec, (qi % 4) * 128:(qi % 4 + 1) * 128],
                            T["K"][:, ec, kb * 512:(kb + 1) * 512],
                            start=(ec == 0), stop=(ec == 7)),
                            reads=[f"Q{ec}", f"K{ec}_{kb}", ptr], writes=[ptr])
                    if last:
                        j = qi % 4
                        P.op("vector", lambda eng, pt=pt, j=j: eng.scalar_tensor_tensor(
                            pt[:], T["c_cm"][:, 384 - j * 128:896 - j * 128],
                            NEG, pt[:], ALU.mult, ALU.add),
                            reads=["c_cm", ptr], writes=[ptr])
                flush()
                for kb, (pt, ptr) in enumerate(ltiles):
                    P.op("vector", lambda eng, pt=pt, kb=kb: eng.reduce_max(
                        T["mxp"][:, kb:kb + 1], pt[:], AX, negate=True),
                        reads=[ptr], writes=[f"mxp{kb}"])
                pb = [(0, min(kbs, 2))] if passes == 1 else [(0, 2), (2, kbs)]
                for pi, (k0, k1) in enumerate(pb):
                    mb = T["m0"] if pi == 0 else T["m1"]
                    mbr = "m0" if pi == 0 else "m1"
                    if k1 - k0 == 2:
                        P.op("vector", lambda eng, mb=mb, k0=k0: eng.tensor_tensor(
                            mb[:], T["mxp"][:, k0:k0 + 1], T["mxp"][:, k0 + 1:k0 + 2],
                            ALU.min), reads=[f"mxp{k0}", f"mxp{k0 + 1}"], writes=[mbr])
                    else:
                        P.op("vector", lambda eng, mb=mb, k0=k0: eng.tensor_copy(
                            mb[:], T["mxp"][:, k0:k0 + 1]), reads=[f"mxp{k0}"], writes=[mbr])
                    for kb in range(k0, k1):
                        pt, ptr = ltiles[kb]
                        P.op("scalar", lambda eng, pt=pt, kb=kb, mb=mb, mbr=mbr: eng.activation(
                            T["Pb"][:, kb * 512:(kb + 1) * 512], pt[:], AF.Exp,
                            bias=mb[:, 0:1],
                            accum_out=T[f"sp{kb}"][:]),
                            reads=[ptr, mbr], writes=[f"Pb{kb}", f"sp{kb}"])
                if passes == 1:
                    if kbs == 2:
                        P.op("vector", lambda eng: eng.tensor_tensor(
                            T["S1"][:], T["sp0"][:], T["sp1"][:], ALU.add),
                            reads=["sp0", "sp1"], writes=["S1"])
                    else:
                        P.op("vector", lambda eng: eng.tensor_copy(
                            T["S1"][:], T["sp0"][:]), reads=["sp0"], writes=["S1"])
                    P.op("vector", lambda eng: eng.reciprocal(
                        T["rS"][:], T["S1"][:]), reads=["S1"], writes=["rS"])
                    P.op("vector", lambda eng: eng.tensor_scalar(
                        T["db0"][:], T["c_i128bf"][:], T["rS"][:, 0:1], None,
                        ALU.mult), reads=["rS", "c_i128bf", "db0"], writes=["db0"])
                else:
                    P.op("vector", lambda eng: eng.tensor_tensor(
                        T["negm"][:], T["m0"][:], T["m1"][:], ALU.min),
                        reads=["m0", "m1"], writes=["negm"])
                    P.op("scalar", lambda eng: eng.activation(
                        T["cp0"][:], T["m0"][:], AF.Exp, bias=T["negm"][:, 0:1],
                        scale=-1.0, accum_out=T["junk1"][:]),
                        reads=["m0", "negm"], writes=["cp0"])
                    P.op("scalar", lambda eng: eng.activation(
                        T["cp1"][:], T["m1"][:], AF.Exp, bias=T["negm"][:, 0:1],
                        scale=-1.0, accum_out=T["junk1"][:]),
                        reads=["m1", "negm"], writes=["cp1"])
                    P.op("vector", lambda eng: eng.tensor_tensor(
                        T["t16a"][:, 0:1], T["sp0"][:], T["sp1"][:], ALU.add),
                        reads=["sp0", "sp1"], writes=["t16a"])
                    if kbs == 4:
                        P.op("vector", lambda eng: eng.tensor_tensor(
                            T["t16b"][:, 0:1], T["sp2"][:], T["sp3"][:], ALU.add),
                            reads=["sp2", "sp3"], writes=["t16b"])
                    else:
                        P.op("vector", lambda eng: eng.tensor_copy(
                            T["t16b"][:, 0:1], T["sp2"][:]),
                            reads=["sp2"], writes=["t16b"])
                    P.op("vector", lambda eng: eng.tensor_tensor(
                        T["t16a"][:, 0:1], T["t16a"][:, 0:1], T["cp0"][:], ALU.mult),
                        reads=["t16a", "cp0"], writes=["t16a"])
                    P.op("vector", lambda eng: eng.scalar_tensor_tensor(
                        T["S1"][:], T["t16b"][:, 0:1], 1.0, T["cp1"][:],
                        ALU.mult, ALU.mult), reads=["t16b", "cp1"], writes=["S1"])
                    P.op("vector", lambda eng: eng.tensor_tensor(
                        T["S1"][:], T["S1"][:], T["t16a"][:, 0:1], ALU.add),
                        reads=["S1", "t16a"], writes=["S1"])
                    P.op("vector", lambda eng: eng.reciprocal(
                        T["rS"][:], T["S1"][:]), reads=["S1"], writes=["rS"])
                    P.op("vector", lambda eng: eng.tensor_tensor(
                        T["d0"][:], T["cp0"][:], T["rS"][:], ALU.mult),
                        reads=["cp0", "rS"], writes=["d0"])
                    P.op("vector", lambda eng: eng.tensor_tensor(
                        T["d1"][:], T["cp1"][:], T["rS"][:], ALU.mult),
                        reads=["cp1", "rS"], writes=["d1"])
                    P.op("vector", lambda eng: eng.tensor_scalar(
                        T["db0"][:], T["c_i128bf"][:], T["d0"][:, 0:1], None,
                        ALU.mult), reads=["d0", "c_i128bf", "db0"], writes=["db0"])
                    P.op("vector", lambda eng: eng.tensor_scalar(
                        T["db1"][:], T["c_i128bf"][:], T["d1"][:, 0:1], None,
                        ALU.mult), reads=["d1", "c_i128bf", "db1"], writes=["db1"])

                def make_ptav(qi=qi, h=h):
                    def emit_ptav():
                        nkc = qi + 1
                        for kcg in range((nkc + 3) // 4):
                            pt, ptr = ptpool.alloc()
                            k0 = kcg * 4
                            kn = min(4, nkc - k0)
                            for j in range(kn):
                                kc = k0 + j
                                db = "db0" if kc < 8 else "db1"
                                P.op("tensor", lambda eng, pt=pt, j=j, kc=kc, db=db: eng.matmul(
                                    pt[:, j * 128:(j + 1) * 128],
                                    T["Pb"][:, kc * 128:(kc + 1) * 128], T[db][:],
                                    start=True, stop=True),
                                    reads=[f"Pb{kc // 4}", db, ptr], writes=[ptr])
                            P.op("vector", lambda eng, pt=pt, k0=k0, kn=kn: eng.tensor_copy(
                                T["PT"][:, k0 * 128:(k0 + kn) * 128],
                                pt[:, 0:kn * 128]), reads=[ptr], writes=[f"PT{k0 // 4}"])
                        for kc in range(nkc):
                            P.op("tensor", lambda eng, kc=kc, qi=qi: eng.matmul(
                                pband[7][0:64, (qi % 4) * 128:(qi % 4 + 1) * 128],
                                T["V"][:, kc, :], T["PT"][:, kc * 128:(kc + 1) * 128],
                                start=(kc == 0), stop=(kc == qi)),
                                reads=[f"V{kc // 8}", f"PT{kc // 4}", "ps7"], writes=["ps7"])
                        if qi % 4 == 3:
                            po = (h % 2) * 64
                            P.op("scalar", lambda eng, qi=qi, h=h, po=po: eng.copy(
                                T["sa"][po:po + 64, h // 2, (qi - 3) * 128:(qi + 1) * 128],
                                pband[7][0:64, :]), reads=["ps7", "sa"], writes=["sa"])
                    return emit_ptav
                if deferred[0] is not None:
                    flush()
                deferred[0] = make_ptav()
            flush()
        P.arena_view("wpb", *WP_VIEW["wpb"])
        for dc in range(8):
            dma("gpsimd", "wp", T["wpb"][:, dc, :], dr["wp"][dc], [], ["wpb"])
        for c in range(NCH):
            xr = T["xr0"] if c % 2 == 0 else T["xr1"]
            xrr = f"xr{c % 2}"
            dma("sync", f"xrs{c % 2}", xr[:], x1_dr[c], [f"x1:{c}"], [xrr])
            ou = T["ou0"] if c % 2 == 0 else T["ou1"]
            our = f"ou{c % 2}"
            for hd in range(2):
                pt, ptr = pool.alloc()
                for dc in range(8):
                    P.op("tensor", lambda eng, pt=pt, dc=dc, c=c, hd=hd: eng.matmul(
                        pt[:], T["sa"][:, dc, c * 128:(c + 1) * 128],
                        T["wpb"][:, dc, hd * 512:(hd + 1) * 512],
                        start=(dc == 0), stop=False),
                        reads=["sa", "wpb", ptr], writes=[ptr])
                P.op("tensor", lambda eng, pt=pt, hd=hd: eng.matmul(
                    pt[:], T["c_ones1bf"][:], T["c_bp"][:, hd * 512:(hd + 1) * 512],
                    start=False, stop=True),
                    reads=["c_ones1bf", "c_bp", ptr], writes=[ptr])
                P.op("vector", lambda eng, pt=pt, ou=ou, hd=hd, xr=xr: eng.tensor_tensor(
                    ou[:, hd * 512:(hd + 1) * 512], pt[:],
                    xr[:, hd * 512:(hd + 1) * 512], ALU.add),
                    reads=[ptr, xrr, our], writes=[our])
            dma("sync", f"st{c % 2}", x2_dr[c], ou[:], [our], [f"x2:{c}"])

    alloc("mxp", [128, 4])
    if phases == ("cvt",):
        for c in range(NCH):
            dma("sync", "dbg", out_dr[c], dr["x0"][c], [], [f"out:{c}"])
    elif phases == ("lnonly",):
        ffn_views()
        if "light" not in dbg_skip:
            load_light_weights()
        ln_phase(dr["x0"], "", "s" not in dbg_skip)
        for c in range(NCH):
            dma("sync", "dbg", out_dr[c], dr["x0"][c], [], [f"out:{c}"])
    elif phases == ("ffn1",):
        ffn_phase(dr["x0"], "", out_dr, "out:", convert=True)
    elif phases == ("ffn1", "attn"):
        ffn_phase(dr["x0"], "", x1_dr, "x1:", convert=True)
        attn_phase()
        for c in range(NCH):
            dma("sync", "dbg", out_dr[c], x2_dr[c], [f"x2:{c}"], [f"out:{c}"])
    else:
        ffn_phase(dr["x0"], "", x1_dr, "x1:", convert=True)
        attn_phase()
        ffn_phase(x2_dr, "x2:", out_dr, "out:")

    P.finalize(nc)
    return nc



def _host_prep(inputs):
    f32 = np.float32
    g = lambda k: np.ascontiguousarray(np.asarray(inputs[k], f32))
    Wkqv = g("L0_attn_Wkqv")
    bkqv = g("L0_attn_bkqv")
    d = {}
    d["wkqv"] = np.ascontiguousarray(Wkqv.reshape(H, 8, 128, 2112))
    d["wp"] = np.ascontiguousarray(g("L0_attn_Wp").reshape(8, 128, D))
    d["bkq"] = np.ascontiguousarray(np.concatenate(
        [bkqv[:, 0:1024].reshape(H, 8, 128).transpose(2, 0, 1).reshape(128, 128),
         bkqv[:, 1024:2048].reshape(H, 8, 128).transpose(2, 0, 1).reshape(128, 128)],
        axis=1))
    d["bv"] = np.ascontiguousarray(bkqv[:, 2048:2112].reshape(1, H * DH))
    d["bp"] = np.ascontiguousarray(g("L0_attn_bp").reshape(1, D))
    d["w1l"] = np.ascontiguousarray(
        g("L0_ffn_w1_l").T.reshape(8, 128, 4, 128).transpose(2, 1, 0, 3))
    d["w2l"] = np.ascontiguousarray(g("L0_ffn_w2_l").T.reshape(4, 128, D))
    d["w1h"] = np.ascontiguousarray(
        g("L0_ffn_w1_h").T.reshape(8, 128, 32, 128).transpose(2, 1, 0, 3))
    d["w2h"] = np.ascontiguousarray(g("L0_ffn_w2_h").T.reshape(32, 128, D))
    d["b1h"] = np.ascontiguousarray(g("L0_ffn_b1_h").reshape(32, 128).T)
    d["b1l"] = np.ascontiguousarray(g("L0_ffn_b1_l").reshape(4, 128).T)
    d["b2l"] = np.ascontiguousarray(g("L0_ffn_b2_l").reshape(1, D))
    d["b2h"] = np.ascontiguousarray(g("L0_ffn_b2_h").reshape(1, D))
    d["gl"] = np.ascontiguousarray(g("L0_ffn_gamma_l").reshape(8, 128).T)
    d["gh"] = np.ascontiguousarray(g("L0_ffn_gamma_h").reshape(8, 128).T)
    d["rrep"] = np.ascontiguousarray(np.broadcast_to(
        g("L0_ffn_routing_token").reshape(1, D), (128, D)))
    cm = np.zeros((128, 896), f32)
    mm_ = np.arange(896)
    for p in range(128):
        cm[p, mm_ > p + 384] = 1.0
    d["cmask"] = cm.astype(BFNP)
    d["dneg"] = (np.eye(128, dtype=f32) * -65536.0).astype(BFNP)
    d["i128bf"] = np.eye(128, dtype=BFNP)
    d["i128f"] = np.eye(128, dtype=f32)
    d["ones1bf"] = np.ones((1, 128), BFNP)
    d["ones1f"] = np.ones((1, 128), np.float32)
    d["ones128f"] = np.ones((128, 128), f32)
    d["l128f"] = np.triu(np.ones((128, 128), f32), 1)
    d["i16f"] = np.eye(16, dtype=f32)
    d["l16f"] = np.triu(np.ones((16, 16), f32), 1)
    return d


def kernel(**inputs):
    host = _host_prep(inputs)
    xfull = np.ascontiguousarray(np.asarray(inputs["L0_attn_x"], np.float32))
    nc = build_nc()
    in_maps = []
    for b in range(NCORES):
        m = dict(host)
        m["x0"] = np.ascontiguousarray(xfull[b].reshape(NCH, 128, D))
        in_maps.append(m)
    res = run_bass_kernel_spmd(nc, in_maps, core_ids=list(range(NCORES)))
    out = np.stack([np.asarray(res.results[b]["out"]).reshape(N, D)
                    for b in range(NCORES)])
    return out
